# version 05
# baseline (speedup 1.0000x reference)
import os
import numpy as np
import ml_dtypes
import concourse.bass as bass
import concourse.mybir as mybir
from concourse.bass_utils import run_bass_kernel_spmd
from contextlib import ExitStack

F32 = mybir.dt.float32
BF16 = mybir.dt.bfloat16
I32 = mybir.dt.int32
AF = mybir.ActivationFunctionType
ALU = mybir.AluOpType
AX = mybir.AxisListType

D = 1024
NJ = 8
R = 32
NL = 6
EPS = 1e-5
ISQ2 = float(1.0 / np.sqrt(2.0))


def build(S):
    PH = int(os.environ.get("KPH", "3"))
    assert S % 512 == 0
    NT = S // 128
    NC2 = S // 256
    NC5 = S // 512
    nc = bass.Bass("TRN2", target_bir_lowering=False)

    x_h = nc.dram_tensor("x", [S, D], F32, kind="ExternalInput").ap()
    wq_h = nc.dram_tensor("wq", [D, D], F32, kind="ExternalInput").ap()
    wk_h = nc.dram_tensor("wk", [D, D], F32, kind="ExternalInput").ap()
    wv_h = nc.dram_tensor("wv", [D, D], F32, kind="ExternalInput").ap()
    wo_h = nc.dram_tensor("wo", [D, D], F32, kind="ExternalInput").ap()
    qw_h = nc.dram_tensor("qw", [NL, D, D], I32, kind="ExternalInput").ap()
    sc_h = nc.dram_tensor("sc", [NL, D, 64], F32, kind="ExternalInput").ap()
    bi_h = nc.dram_tensor("bi", [NL, 1, D], F32, kind="ExternalInput").ap()
    la_h = nc.dram_tensor("la", [NL, R, D], F32, kind="ExternalInput").ap()
    lb_h = nc.dram_tensor("lb", [NL, D, R], F32, kind="ExternalInput").ap()
    gm_h = nc.dram_tensor("gm", [3, 1, D], F32, kind="ExternalInput").ap()
    bt_h = nc.dram_tensor("bt", [3, 1, D], F32, kind="ExternalInput").ap()
    idb_h = nc.dram_tensor("idb", [128, 128], BF16, kind="ExternalInput").ap()
    idf_h = nc.dram_tensor("idf", [128, 128], F32, kind="ExternalInput").ap()
    cm_h = nc.dram_tensor("cm", [128, 384], BF16, kind="ExternalInput").ap()
    out_h = nc.dram_tensor("out", [S, D], F32, kind="ExternalOutput").ap()
    qTs = nc.dram_tensor("qTs", [D, S], BF16, kind="Internal").ap()
    vs_h = nc.dram_tensor("vs", [S, D], BF16, kind="Internal").ap()
    x2s = nc.dram_tensor("x2s", [S, D], F32, kind="Internal").ap()
    hTs = nc.dram_tensor("hTs", [D, S], BF16, kind="Internal").ap()

    STREAMS = ("t", "v", "s", "g", "q")
    DMACH = ("qc", "xt0", "xt1", "wf0", "wf1", "wo0", "wo1", "qb0", "qb1",
             "xr0", "xr1", "gsv0", "gsv1", "gsq0", "gsq1",
             "gv0", "gv1", "gv2", "gv3", "go",
             "sd0", "sd1", "hs0", "hs1", "qw", "sc2", "lo", "lb", "bi", "gb", "hb", "x2r")
    ops = {e: [] for e in STREAMS}
    cnt = {e: 0 for e in STREAMS + DMACH}

    def emit(stream, fn, deps=(), strict=(), dmach=None):
        ch = dmach or stream
        inc = 16 if dmach else 1
        cnt[ch] += 1
        tok = (ch, cnt[ch] * inc)
        ds = tuple(d for d in deps
                   if d is not None and (d[0] != ch or ch in ("t", "v", "s")))
        st = tuple(d for d in strict if d is not None)
        ops[stream].append((ds, st, ch, fn, inc))
        return tok

    def waitop(stream, deps):
        ops[stream].append((tuple(d for d in deps if d is not None), (), None, None, 0))

    NSLOT = 32
    slot_read_toks = [[] for _ in range(NSLOT)]
    slot_state = {"i": 0}

    with ExitStack() as OUTER:
        EO = OUTER.enter_context
        sems = {ch: EO(nc.semaphore("sem_" + ch)) for ch in cnt}
        ident = EO(nc.sbuf_tensor("ident", [128, 128], BF16))
        identf = EO(nc.sbuf_tensor("identf", [128, 128], F32))
        cmask = EO(nc.sbuf_tensor("cmask", [128, 384], BF16))
        stp = EO(nc.sbuf_tensor("stp", [128, NSLOT * 8], F32))
        sqd = EO(nc.sbuf_tensor("sqd", [128, 1024], BF16))
        ones1 = EO(nc.sbuf_tensor("ones1", [1, 512], BF16))
        onesc = EO(nc.sbuf_tensor("onesc", [128, 1], BF16))
        brow = EO(nc.sbuf_tensor("brow", [1, NL * 1024], BF16))
        laT = EO(nc.sbuf_tensor("laT", [128, NL * 256], BF16))
        lbT = EO(nc.sbuf_tensor("lbT", [32, NL * 1024], BF16))
        rsrow = EO(nc.sbuf_tensor("rsrow", [1, 256], F32))
        rs_col = EO(nc.sbuf_tensor("rs_col", [128, 2], F32))
        PB0 = EO(nc.psum_tensor("PB0", [128, 1024], F32))
        PB1 = EO(nc.psum_tensor("PB1", [128, 1024], F32))
        PB2 = EO(nc.psum_tensor("PB2", [128, 1024], F32))
        PB3 = EO(nc.psum_tensor("PB3", [128, 1024], BF16))
        PB4 = EO(nc.psum_tensor("PB4", [128, 512], F32))

        emit("q", lambda e: e.dma_start(out=ident[:], in_=idb_h), dmach="qc")
        emit("q", lambda e: e.dma_start(out=identf[:], in_=idf_h), dmach="qc")
        t_id = t_idf = t_cm = emit(
            "q", lambda e: e.dma_start(out=cmask[:], in_=cm_h), dmach="qc")
        t_ones1 = emit("v", lambda e: e.memset(ones1[:], 1.0))
        t_onesc = emit("v", lambda e: e.memset(onesc[:], 1.0))

        def ln_stats(src_ap, W, deps):
            k = slot_state["i"] % NSLOT
            slot_state["i"] += 1
            war = tuple(slot_read_toks[k])
            slot_read_toks[k] = []
            c0 = k * 8
            m_ap = stp[:, c0:c0 + 1]
            rs_ap = stp[:, c0 + 1:c0 + 2]
            nm_ap = stp[:, c0 + 2:c0 + 3]
            sq_ap = stp[:, c0 + 3:c0 + 4]
            m2_ap = stp[:, c0 + 4:c0 + 5]
            deps = tuple(deps)
            t1 = emit("v", lambda e: e.reduce_sum(out=m_ap, in_=src_ap, axis=AX.X),
                      deps + war)
            t2 = emit("s", lambda e: e.activation(out=sqd[:, 0:W], in_=src_ap,
                                                  func=AF.Square, accum_out=sq_ap),
                      deps + war + (t1,))
            t3 = emit("v", lambda e: e.tensor_scalar_mul(out=m_ap, in0=m_ap,
                                                         scalar1=1.0 / W),
                      strict=(t1,))
            t4 = emit("s", lambda e: e.square(out=m2_ap, in_=m_ap), (t3,), strict=(t2,))
            t5 = emit("v", lambda e: e.scalar_tensor_tensor(
                out=rs_ap, in0=sq_ap, scalar=1.0 / W, in1=m2_ap,
                op0=ALU.mult, op1=ALU.subtract), (t2, t4))
            t5b = emit("v", lambda e: e.tensor_scalar_add(out=rs_ap, in0=rs_ap,
                                                          scalar1=EPS),
                       strict=(t5,))
            t6 = emit("s", lambda e: e.activation(out=rs_ap, in_=rs_ap,
                                                  func=AF.Sqrt), (t5b,))
            t7 = emit("v", lambda e: e.reciprocal(out=rs_ap, in_=rs_ap), (t6,))
            t8 = emit("v", lambda e: e.scalar_tensor_tensor(
                out=nm_ap, in0=m_ap, scalar=-1.0, in1=rs_ap,
                op0=ALU.mult, op1=ALU.mult), (t3,), strict=(t7,))

            def reg(tok):
                slot_read_toks[k].append(tok)
            reg(t8)
            return m_ap, rs_ap, nm_ap, t3, t7, t8, reg, t2

        with ExitStack() as AB:
            EAB = AB.enter_context
            kT = EAB(nc.sbuf_tensor("kT", [128, NJ * S], BF16))
            kT_done = [None]

            with ExitStack() as A:
                EA = A.enter_context
                xnT = EA(nc.sbuf_tensor("xnT", [128, NJ * S], BF16))
                wf32 = EA(nc.sbuf_tensor("wf32", [128, 2 * 1024], F32))
                wbuf = EA(nc.sbuf_tensor("wbuf", [128, NJ * 1024], BF16))
                xtile = EA(nc.sbuf_tensor("xtile", [128, 2 * 1024], F32))
                xn_sb = EA(nc.sbuf_tensor("xn_sb", [128, 2 * 1024], BF16))
                vst = EA(nc.sbuf_tensor("vst", [128, 2 * 1024], BF16))
                qst = EA(nc.sbuf_tensor("qst", [128, 2 * 512], BF16))
                pP = PB0
                pTa = PB3

                xt_rd = [None, None]
                xn_rd = [None, None]
                pTa_rd = [None]
                xnT_tok = {}
                for tt in range(NT):
                    sl = tt % 2
                    xa = xtile[:, sl * 1024:(sl + 1) * 1024]
                    tld = emit("q", lambda e, tt=tt, xa=xa: e.dma_start(
                        out=xa, in_=x_h[tt * 128:(tt + 1) * 128, :]),
                        deps=(xt_rd[sl],), dmach="xt%d" % sl)
                    m_ap, rs_ap, nm_ap, tm, trs, tnm, reg, tsq = ln_stats(xa, 1024, (tld,))
                    na = xn_sb[:, sl * 1024:(sl + 1) * 1024]
                    tap = emit("s", lambda e, xa=xa, na=na, rs_ap=rs_ap, nm_ap=nm_ap:
                               e.activation(out=na, in_=xa, func=AF.Identity,
                                            scale=rs_ap, bias=nm_ap),
                               (tnm, xn_rd[sl]), strict=(trs,))
                    reg(tap)
                    xt_rd[sl] = tap
                    last_ev = None
                    for j in range(NJ):
                        psl = j % 2
                        pa = pTa[:, psl * 128:(psl + 1) * 128]
                        ttr = emit("t", lambda e, na=na, j=j, pa=pa: e.transpose(
                            pa, na[:, j * 128:(j + 1) * 128], ident[:]),
                            (tap, t_id, pTa_rd[0]))
                        dst = xnT[:, j * S + tt * 128: j * S + (tt + 1) * 128]
                        tev = emit("s", lambda e, pa=pa, dst=dst: e.copy(out=dst, in_=pa),
                                   (ttr,))
                        pTa_rd[0] = tev
                        xnT_tok[(j, tt)] = tev
                        last_ev = tev
                    xn_rd[sl] = last_ev
                    if tt == 0:
                        snap10 = {e: len(ops[e]) for e in STREAMS}
                xnT_all = last_ev

                def xnT_ready(tc_lo, tc_hi):
                    return xnT_tok[(NJ - 1, tc_hi - 1)]


                snap11 = {e: len(ops[e]) for e in STREAMS}
                wf_rd = [None, None]
                def load_wproj(w_hbm, deps):
                    toks = []
                    for j in range(NJ):
                        sl = j % 2
                        wa = wf32[:, sl * 1024:(sl + 1) * 1024]
                        tl = emit("q", lambda e, wa=wa, j=j, w_hbm=w_hbm: e.dma_start(
                            out=wa, in_=w_hbm[j * 128:(j + 1) * 128, :]),
                            deps=(wf_rd[sl],) + tuple(deps), dmach="wf%d" % sl)
                        tc = emit("v", lambda e, wa=wa, j=j: e.tensor_copy(
                            out=wbuf[:, j * 1024:(j + 1) * 1024], in_=wa), (tl,))
                        wf_rd[sl] = tc
                        toks.append(tc)
                    return toks

                wk_toks = load_wproj(wk_h, ())
                spill_fin = {}
                pP_rd = [None, None]
                kT_tok = {}
                for m in range(NJ):
                    for tc in range(S // 512):
                        psl = (m * (S // 512) + tc) % 2
                        pa = pP[:, psl * 512:(psl + 1) * 512]
                        dep0 = (xnT_tok[(NJ - 1, tc * 4 + 3)],
                                pP_rd[psl]) + tuple(wk_toks)
                        tmm = None
                        for j in range(NJ):
                            tmm = emit("t", lambda e, j=j, m=m, tc=tc, pa=pa: e.matmul(
                                pa,
                                wbuf[:, j * 1024 + m * 128: j * 1024 + (m + 1) * 128],
                                xnT[:, j * S + tc * 512: j * S + (tc + 1) * 512],
                                start=(j == 0), stop=(j == NJ - 1)),
                                dep0 if j == 0 else ())
                        dst = kT[:, m * S + tc * 512: m * S + (tc + 1) * 512]
                        tev = emit("s", lambda e, pa=pa, dst=dst: e.copy(out=dst, in_=pa),
                                   (tmm,))
                        pP_rd[psl] = tev
                        kT_tok[(m, tc)] = tev
                kT_done[0] = tev

                snap12 = {e: len(ops[e]) for e in STREAMS}
                wv_toks = load_wproj(wv_h, (kT_done[0],))
                vst_rd = [None, None]
                for tt in range(NT):
                    sl = tt % 2
                    for dv in range(2):
                        psl = dv
                        pa = pP[:, psl * 512:(psl + 1) * 512]
                        dep0 = (pP_rd[psl], vst_rd[sl]) + tuple(wv_toks)
                        tmm = None
                        for j in range(NJ):
                            tmm = emit("t", lambda e, j=j, tt=tt, dv=dv, pa=pa: e.matmul(
                                pa,
                                xnT[:, j * S + tt * 128: j * S + (tt + 1) * 128],
                                wbuf[:, j * 1024 + dv * 512: j * 1024 + (dv + 1) * 512],
                                start=(j == 0), stop=(j == NJ - 1)),
                                dep0 if j == 0 else ())
                        dst = vst[:, sl * 1024 + dv * 512: sl * 1024 + (dv + 1) * 512]
                        tev = emit("s", lambda e, pa=pa, dst=dst: e.copy(out=dst, in_=pa),
                                   (tmm,))
                        pP_rd[psl] = tev
                        if dv == 0:
                            tev0 = tev
                    tsp = emit("g", lambda e, tt=tt, sl=sl: e.dma_start(
                        out=vs_h[tt * 128:(tt + 1) * 128, :],
                        in_=vst[:, sl * 1024:(sl + 1) * 1024]), (tev0, tev),
                        dmach="gsv%d" % sl)
                    vst_rd[sl] = tsp
                    spill_fin[tsp[0]] = tsp
                v_done = tev

                snap13 = {e: len(ops[e]) for e in STREAMS}
                wq_toks = load_wproj(wq_h, (v_done,))
                qst_rd = [None, None]
                for m in range(NJ):
                    for tc in range(S // 512):
                        psl = (m * (S // 512) + tc) % 2
                        sl = (m * (S // 512) + tc) % 2
                        pa = pP[:, psl * 512:(psl + 1) * 512]
                        dep0 = (pP_rd[psl], qst_rd[sl]) + tuple(wq_toks)
                        tmm = None
                        for j in range(NJ):
                            tmm = emit("t", lambda e, j=j, m=m, tc=tc, pa=pa: e.matmul(
                                pa,
                                wbuf[:, j * 1024 + m * 128: j * 1024 + (m + 1) * 128],
                                xnT[:, j * S + tc * 512: j * S + (tc + 1) * 512],
                                start=(j == 0), stop=(j == NJ - 1)),
                                dep0 if j == 0 else ())
                        dst = qst[:, sl * 512:(sl + 1) * 512]
                        tev = emit("s", lambda e, pa=pa, dst=dst: e.copy(out=dst, in_=pa),
                                   (tmm,))
                        pP_rd[psl] = tev
                        tsp = emit("g", lambda e, m=m, tc=tc, dst=dst: e.dma_start(
                            out=qTs[m * 128:(m + 1) * 128, tc * 512:(tc + 1) * 512],
                            in_=dst), (tev,), dmach="gsq%d" % sl)
                        qst_rd[sl] = tsp
                        spill_fin[tsp[0]] = tsp
                spill_final = tuple(spill_fin.values())
            snap1 = {e: len(ops[e]) for e in STREAMS}
            with ExitStack() as B:
                EB = B.enter_context
                qbuf = EB(nc.sbuf_tensor("qbuf", [128, 2 * NJ * 256], BF16))
                vbuf = EB(nc.sbuf_tensor("vbuf", [128, 4 * 1024], BF16))
                expT = EB(nc.sbuf_tensor("expT", [128, 4 * 256], BF16))
                attn_sb = EB(nc.sbuf_tensor("attn_sb", [128, 2 * 1024], BF16))
                attnT = EB(nc.sbuf_tensor("attnT", [128, NJ * 256], BF16))
                wo32 = EB(nc.sbuf_tensor("wo32", [128, 2 * 1024], F32))
                wo_sb = EB(nc.sbuf_tensor("wo_sb", [128, NJ * 1024], BF16))
                x_res = EB(nc.sbuf_tensor("x_res", [128, 2 * 1024], F32))
                x2_sb = EB(nc.sbuf_tensor("x2_sb", [128, 2 * 1024], F32))
                x2n_sb = EB(nc.sbuf_tensor("x2n_sb", [128, 2 * 1024], BF16))
                xnT_st = EB(nc.sbuf_tensor("xnT_st", [128, 2 * NJ * 256], BF16))
                lnb = EB(nc.sbuf_tensor("lnb", [128, 1024], F32))
                lnsrc = EB(nc.sbuf_tensor("lnsrc", [128, 2 * 1024], F32))
                pAV0 = PB0
                pAV1 = PB1
                pS = PB2
                pRS_ap = PB4[0:1, 256:512]
                pTb = PB3
                pTrs = PB4

                A_done = emit("v", lambda e: e.memset(stp[:, 255:256], 0.0),
                              spill_final + (kT_done[0],))

                wo_toks = []
                wo_rd2 = [A_done, A_done]
                for j in range(NJ):
                    sl = j % 2
                    wa = wo32[:, sl * 1024:(sl + 1) * 1024]
                    tl = emit("q", lambda e, wa=wa, j=j: e.dma_start(
                        out=wa, in_=wo_h[j * 128:(j + 1) * 128, :]),
                        deps=(wo_rd2[sl],), dmach="wo%d" % sl)
                    tc_ = emit("v", lambda e, wa=wa, j=j: e.tensor_copy(
                        out=wo_sb[:, j * 1024:(j + 1) * 1024], in_=wa), (tl,))
                    wo_rd2[sl] = tc_
                    wo_toks.append(tc_)

                pAVs = [pAV0, pAV1]
                qb_rd = [A_done, A_done]
                vb_rd = [A_done] * 4
                exp_rd = [A_done] * 4
                pS_rd = [A_done, A_done]
                pAV_rd = [A_done, A_done]
                pRS_rd = A_done
                rsrow_rd = A_done
                rscol_rd = A_done
                attn_rd = [A_done, A_done]
                attnT_rd = A_done
                xres_rd = [A_done, A_done]
                x2_rd = [A_done, A_done]
                lnsrc_rd = [A_done, A_done]
                x2sp_rd = [None, None]
                x2n_rd = [A_done, A_done]
                xst_rd = [A_done, A_done]
                lnb_rd = A_done
                vb_ctr = [0]
                snap2l = []
                x2_spill_toks = []
                hT_spill_toks = []

                for c in range(NC2):
                    s0 = c * 256
                    qsl = c % 2
                    q_toks = []
                    for m in range(NJ):
                        dst = qbuf[:, qsl * NJ * 256 + m * 256: qsl * NJ * 256 + (m + 1) * 256]
                        tq = emit("q", lambda e, m=m, s0=s0, dst=dst: e.dma_start(
                            out=dst, in_=qTs[m * 128:(m + 1) * 128, s0:s0 + 256]),
                            spill_final + (qb_rd[qsl],), dmach="qb%d" % qsl)
                        q_toks.append(tq)
                    n_tt = 2 * c + 2
                    t_score_last = None
                    t_rs_last = None
                    t_av_last = None
                    pend = None
                    for tt in range(n_tt + 1):
                        if tt < n_tt:
                            vsl = vb_ctr[0] % 4
                            vb_ctr[0] += 1
                            va = vbuf[:, vsl * 1024:(vsl + 1) * 1024]
                            tv = emit("g", lambda e, tt=tt, va=va: e.dma_start(
                                out=va, in_=vs_h[tt * 128:(tt + 1) * 128, :]),
                                spill_final + (vb_rd[vsl],), dmach="gv%d" % vsl)
                            psl = tt % 2
                            pa = pS[:, psl * 512: psl * 512 + 256]
                            dep0 = tuple(q_toks) + (pS_rd[psl], kT_done[0])
                            tmm = None
                            for m in range(NJ):
                                tmm = emit("t", lambda e, m=m, tt=tt, c=c, pa=pa,
                                           qsl=qsl: e.matmul(
                                    pa,
                                    kT[:, m * S + tt * 128: m * S + (tt + 1) * 128],
                                    qbuf[:, qsl * NJ * 256 + m * 256: qsl * NJ * 256 + (m + 1) * 256],
                                    start=(m == 0), stop=(m == NJ - 1)),
                                    dep0 if m == 0 else ())
                            t_score_last = tmm
                            esl = tt % 4
                            ea = expT[:, esl * 256:(esl + 1) * 256]
                            tex = emit("s", lambda e, pa=pa, ea=ea: e.activation(
                                out=ea, in_=pa, func=AF.Exp, scale=0.03125),
                                (tmm, exp_rd[esl]))
                            pS_rd[psl] = tex
                            tmask = None
                            if tt >= 2 * c:
                                kk = tt - 2 * c
                                cs = 128 - 128 * kk
                                tmask = emit("v", lambda e, ea=ea, cs=cs:
                                             e.tensor_tensor(
                                    out=ea, in0=ea, in1=cmask[:, cs:cs + 256],
                                    op=ALU.mult), (tex, t_cm))
                            cur = (tt, ea, va, tmask or tex, tv, vsl, esl)
                        else:
                            cur = None
                        if pend is not None:
                            ptt, ea, va, tconsume, tv, vsl, esl = pend
                            trs = emit("t", lambda e, ea=ea, ptt=ptt, n_tt=n_tt:
                                       e.matmul(
                                pRS_ap, onesc[:], ea,
                                start=(ptt == 0), stop=(ptt == n_tt - 1)),
                                (tconsume, t_onesc) + ((pRS_rd,) if ptt == 0 else ()))
                            t_rs_last = trs
                            tav = None
                            for ss in range(2):
                                for dv in range(2):
                                    dep = (tconsume, tv)
                                    if ptt == 0:
                                        dep = dep + (pAV_rd[ss],)
                                    tav = emit("t", lambda e, ea=ea, ss=ss, dv=dv,
                                               va=va, ptt=ptt, n_tt=n_tt: e.matmul(
                                        pAVs[ss][:, dv * 512:(dv + 1) * 512],
                                        ea[:, ss * 128:(ss + 1) * 128],
                                        va[:, dv * 512:(dv + 1) * 512],
                                        start=(ptt == 0), stop=(ptt == n_tt - 1)),
                                        dep)
                            t_av_last = tav
                            vb_rd[vsl] = tav
                            exp_rd[esl] = tav
                        pend = cur
                    qb_rd[qsl] = t_score_last
                    if c == 0:
                        snap30 = {e: len(ops[e]) for e in STREAMS}
                    trc = emit("v", lambda e: e.reciprocal(out=rsrow[:], in_=pRS_ap),
                               (t_rs_last, rsrow_rd))
                    ttr0 = emit("t", lambda e: e.transpose(
                        pTrs[:, 0:1], rsrow[0:1, 0:128], identf[0:1, 0:1]),
                        (trc, t_idf, rscol_rd))
                    ttr1 = emit("t", lambda e: e.transpose(
                        pTrs[:, 1:2], rsrow[0:1, 128:256], identf[0:1, 0:1]), (trc,))
                    tec = emit("s", lambda e: e.copy(out=rs_col[:], in_=pTrs[:, 0:2]),
                               (ttr0, ttr1))
                    rsrow_rd = ttr1
                    pRS_rd = tec
                    if c == 0:
                        snap31 = {e: len(ops[e]) for e in STREAMS}
                    tproj = [None, None]
                    for ss in range(2):
                        aa = attn_sb[:, ss * 1024:(ss + 1) * 1024]
                        tn = emit("s", lambda e, ss=ss, aa=aa: e.activation(
                            out=aa, in_=pAVs[ss][:], func=AF.Copy,
                            scale=rs_col[:, ss:ss + 1]),
                            (t_av_last, attn_rd[ss]), strict=(tec,))
                        attn_rd[ss] = tn
                        pAV_rd[ss] = tn
                    rscol_rd = attn_rd[1]
                    if c == 0:
                        snap32 = {e: len(ops[e]) for e in STREAMS}
                    at_evs = []
                    for j in range(NJ):
                        tt0 = emit("t", lambda e, j=j: e.transpose(
                            pTb[:, 0:128],
                            attn_sb[:, 0 * 1024 + j * 128: 0 * 1024 + (j + 1) * 128],
                            ident[:]), (attn_rd[0], attnT_rd))
                        tt1 = emit("t", lambda e, j=j: e.transpose(
                            pTb[:, 128:256],
                            attn_sb[:, 1 * 1024 + j * 128: 1 * 1024 + (j + 1) * 128],
                            ident[:]), (attn_rd[1],))
                        tev = emit("s", lambda e, j=j: e.copy(
                            out=attnT[:, j * 256:(j + 1) * 256], in_=pTb[:, 0:256]),
                            (tt0, tt1))
                        at_evs.append(tev)
                        attnT_rd = tev
                    attn_rd[0] = at_evs[-1]
                    attn_rd[1] = at_evs[-1]
                    if c == 0:
                        snap33 = {e: len(ops[e]) for e in STREAMS}
                    for ss in range(2):
                        dep0 = tuple(at_evs) + tuple(wo_toks) + (pAV_rd[ss],)
                        tmm = None
                        for j in range(NJ):
                            for dc in range(2):
                                tmm = emit("t", lambda e, j=j, dc=dc, ss=ss: e.matmul(
                                    pAVs[ss][:, dc * 512:(dc + 1) * 512],
                                    attnT[:, j * 256 + ss * 128: j * 256 + (ss + 1) * 128],
                                    wo_sb[:, j * 1024 + dc * 512: j * 1024 + (dc + 1) * 512],
                                    start=(j == 0), stop=(j == NJ - 1)),
                                    dep0 if (j == 0 and dc == 0) else ())
                        tproj[ss] = tmm
                    if c == 0:
                        snap34 = {e: len(ops[e]) for e in STREAMS}
                    for ss in range(2):
                        row0 = s0 + ss * 128
                        xsl = ss
                        xa = x_res[:, xsl * 1024:(xsl + 1) * 1024]
                        tlx = emit("q", lambda e, row0=row0, xa=xa: e.dma_start(
                            out=xa, in_=x_h[row0:row0 + 128, :]),
                            (xres_rd[xsl],), dmach="xr%d" % xsl)
                        if c == 0 and ss == 0:
                            snap40 = {e: len(ops[e]) for e in STREAMS}
                        la_ = lnsrc[:, ss * 1024:(ss + 1) * 1024]
                        tcp = emit("s", lambda e, ss=ss, la_=la_: e.copy(
                            out=la_, in_=pAVs[ss][:]),
                            (tproj[ss], lnsrc_rd[ss]))
                        pAV_rd[ss] = tcp
                        m_ap, rs_ap, nm_ap, tm, trs_, tnm, reg, tsq = ln_stats(
                            la_, 1024, (tcp,))
                        if c == 0 and ss == 0:
                            snap41 = {e: len(ops[e]) for e in STREAMS}
                        tap = emit("s", lambda e, la_=la_, rs_ap=rs_ap, nm_ap=nm_ap:
                                   e.activation(out=lnb[:], in_=la_,
                                                func=AF.Identity, scale=rs_ap,
                                                bias=nm_ap),
                                   (tnm, lnb_rd), strict=(trs_,))
                        reg(tap)
                        lnsrc_rd[ss] = tap
                        if c == 0 and ss == 0:
                            snap42 = {e: len(ops[e]) for e in STREAMS}
                        x2a = x2_sb[:, ss * 1024:(ss + 1) * 1024]
                        tx2 = emit("v", lambda e, x2a=x2a, xa=xa: e.tensor_add(
                            out=x2a, in0=lnb[:], in1=xa),
                            (tap, tlx, x2_rd[ss], x2sp_rd[ss]))
                        lnb_rd = tx2
                        xres_rd[xsl] = tx2
                        if c == 0 and ss == 0:
                            snap43 = {e: len(ops[e]) for e in STREAMS}
                        tspx2 = emit("s", lambda e, row0=row0, x2a=x2a: e.dma_start(
                            out=x2s[row0:row0 + 128, :], in_=x2a), (tx2,),
                            dmach="sd%d" % ss)
                        x2_spill_toks.append(tspx2)
                        x2sp_rd[ss] = tspx2
                        if c == 0 and ss == 0:
                            snap44 = {e: len(ops[e]) for e in STREAMS}
                        m2_ap, rs2_ap, nm2_ap, tm2, trs2, tnm2, reg2, _ = ln_stats(
                            x2a, 1024, (tx2,))
                        if c == 0 and ss == 0:
                            snap45 = {e: len(ops[e]) for e in STREAMS}
                        xna = x2n_sb[:, ss * 1024:(ss + 1) * 1024]
                        txn = emit("v", lambda e, x2a=x2a, xna=xna, m2_ap=m2_ap,
                                   rs2_ap=rs2_ap: e.tensor_scalar(
                            out=xna, in0=x2a, scalar1=m2_ap, scalar2=rs2_ap,
                            op0=ALU.subtract, op1=ALU.mult),
                            (x2n_rd[ss],), strict=(tm2, trs2))
                        reg2(txn)
                        x2_rd[ss] = txn
                        if c == 0 and ss == 0:
                            snap46 = {e: len(ops[e]) for e in STREAMS}
                        x2n_rd[ss] = None
                    if c == 0:
                        snap35 = {e: len(ops[e]) for e in STREAMS}
                    stsl = c % 2
                    for j in range(NJ):
                        tt0 = emit("t", lambda e, j=j: e.transpose(
                            pTb[:, 0:128],
                            x2n_sb[:, 0 * 1024 + j * 128: 0 * 1024 + (j + 1) * 128],
                            ident[:]), (x2_rd[0], attnT_rd))
                        tt1 = emit("t", lambda e, j=j: e.transpose(
                            pTb[:, 128:256],
                            x2n_sb[:, 1 * 1024 + j * 128: 1 * 1024 + (j + 1) * 128],
                            ident[:]), (x2_rd[1],))
                        dst = xnT_st[:, stsl * NJ * 256 + j * 256: stsl * NJ * 256 + (j + 1) * 256]
                        tev = emit("s", lambda e, dst=dst: e.copy(out=dst,
                                                                   in_=pTb[:, 0:256]),
                                   (tt0, tt1, xst_rd[stsl] if j == 0 else None))
                        attnT_rd = tev
                        tsp = emit("s", lambda e, j=j, s0=s0, dst=dst: e.dma_start(
                            out=hTs[j * 128:(j + 1) * 128, s0:s0 + 256], in_=dst),
                            (tev,), dmach="hs%d" % stsl)
                        hT_spill_toks.append(tsp)
                        if j == NJ - 1:
                            xst_rd[stsl] = tsp
                            x2n_rd[0] = tev
                            x2n_rd[1] = tev
                    snap2l.append({e: len(ops[e]) for e in STREAMS})
                snap2c = getattr(build, "_snap2c", None)
                x2_spill_all = tuple({t[0]: t for t in x2_spill_toks}.values())
                hT_spill_all = tuple({t[0]: t for t in hT_spill_toks}.values())
        snap2 = {e: len(ops[e]) for e in STREAMS}
        with ExitStack() as C:
            EC = C.enter_context
            wT = EC(nc.sbuf_tensor("wT", [128, NL * NJ * 1024], BF16))
            gmb = EC(nc.sbuf_tensor("gmb", [128, 1024], F32))
            btb = EC(nc.sbuf_tensor("btb", [128, 1024], F32))
            hTb = EC(nc.sbuf_tensor("hTb", [128, NJ * 512], BF16))
            hTf = EC(nc.sbuf_tensor("hTf", [128, NJ * 512], F32))
            g_sb = EC(nc.sbuf_tensor("g_sb", [128, 4 * 1024], BF16))
            gT = EC(nc.sbuf_tensor("gT", [128, NJ * 512], BF16))
            zbuf = EC(nc.sbuf_tensor("zbuf", [128, 1024], F32))
            x2res = EC(nc.sbuf_tensor("x2res", [128, 1024], F32))
            qwt = EC(nc.sbuf_tensor("qwt", [128, 1024], I32))
            dqf = EC(nc.sbuf_tensor("dqf", [128, 1024], F32))
            dqb = EC(nc.sbuf_tensor("dqb", [128, 1024], BF16))
            sct = EC(nc.sbuf_tensor("sct", [128, 64], F32))
            lab2 = EC(nc.sbuf_tensor("lab2", [32, 1024], BF16))
            lb32 = EC(nc.sbuf_tensor("lb32", [128, 32], F32))
            lbb2 = EC(nc.sbuf_tensor("lbb2", [128, 32], BF16))
            raT = EC(nc.sbuf_tensor("raT", [32, 2 * 512], BF16))
            pYs = [PB0, PB1]
            pZ = PB1
            pH = PB2
            pA_ap = PB4[0:32, 0:512]
            pTc = PB3

            B_done = emit("v", lambda e: e.memset(stp[:, 254:255], 0.0),
                          x2_spill_all + hT_spill_all)
            wT_tok = [None] * NL
            qwt_rd = [B_done]
            sct_rd = [B_done]
            dqf_rd = B_done
            dqb_rd = B_done
            pTc_rd = B_done
            for l in range(NL):
                tev_last = None
                for ot in range(NJ):
                    sl = ot % 2
                    qa = qwt[:]
                    tlq = emit("q", lambda e, l=l, ot=ot, qa=qa: e.dma_start(
                        out=qa, in_=qw_h[l, ot * 128:(ot + 1) * 128, :]),
                        (qwt_rd[0],), dmach="qw")
                    sa = sct[:]
                    tls = emit("q", lambda e, l=l, ot=ot, sa=sa: e.dma_start(
                        out=sa, in_=sc_h[l, ot * 128:(ot + 1) * 128, :]),
                        (sct_rd[0],), dmach="sc2")
                    td1 = emit("v", lambda e, qa=qa: e.tensor_scalar(
                        out=dqf[:], in0=qa, scalar1=1.0 / 7.5, scalar2=-1.0,
                        op0=ALU.mult, op1=ALU.add), (tlq, dqf_rd))
                    qwt_rd[0] = td1
                    td2 = emit("v", lambda e, sa=sa: e.tensor_tensor(
                        out=dqb[:].rearrange("p (g j) -> p g j", j=16),
                        in0=dqf[:].rearrange("p (g j) -> p g j", j=16),
                        in1=sa.unsqueeze(2).broadcast_to([128, 64, 16]),
                        op=ALU.mult), (tls, td1, dqb_rd))
                    dqf_rd = td2
                    sct_rd[0] = td2
                    for j in range(NJ):
                        qsl = j % 4
                        ttr = emit("t", lambda e, j=j, qsl=qsl: e.transpose(
                            pTc[:, qsl * 128:(qsl + 1) * 128],
                            dqb[:, j * 128:(j + 1) * 128], ident[:]),
                            (td2, pTc_rd))
                        dst = wT[:, l * NJ * 1024 + j * 1024 + ot * 128:
                                 l * NJ * 1024 + j * 1024 + (ot + 1) * 128]
                        tev = emit("s", lambda e, dst=dst, qsl=qsl: e.copy(
                            out=dst, in_=pTc[:, qsl * 128:(qsl + 1) * 128]), (ttr,))
                        tev_last = tev
                        pTc_rd = tev
                    dqb_rd = tev_last
                wT_tok[l] = tev_last
                if l == 0:
                    snap51 = {e: len(ops[e]) for e in STREAMS}
            snap52 = {e: len(ops[e]) for e in STREAMS}
            laT_tok = [None] * NL
            la_rd = B_done
            for l in range(NL):
                tl = emit("q", lambda e, l=l: e.dma_start(
                    out=dqf[0:32, :], in_=la_h[l]), (la_rd, dqf_rd), dmach="lo")
                tc_ = emit("v", lambda e: e.tensor_copy(out=lab2[:], in_=dqf[0:32, :]),
                           (tl,))
                dqf_rd = tc_
                la_rd = tc_
                evs = []
                for j in range(NJ):
                    qsl = j % 4
                    ttr = emit("t", lambda e, j=j, qsl=qsl: e.transpose(
                        pTc[:, qsl * 32:(qsl + 1) * 32],
                        lab2[:, j * 128:(j + 1) * 128], ident[0:32, 0:32]),
                        (tc_,) + ((pTc_rd,) if qsl == 0 else ()))
                    if qsl == 3:
                        j0 = j - 3
                        dst = laT[:, l * 256 + j0 * 32: l * 256 + (j0 + 4) * 32]
                        tev = emit("s", lambda e, dst=dst: e.copy(
                            out=dst, in_=pTc[:, 0:128]), (ttr,))
                        evs.append(tev)
                        pTc_rd = tev
                la_rd = evs[-1]
                laT_tok[l] = evs[-1]
            snap53 = {e: len(ops[e]) for e in STREAMS}
            lbT_tok = [None] * NL
            lb_rd = B_done
            for l in range(NL):
                evs = []
                for ot in range(NJ):
                    tl = emit("q", lambda e, l=l, ot=ot: e.dma_start(
                        out=lb32[:], in_=lb_h[l, ot * 128:(ot + 1) * 128, :]),
                        (lb_rd,), dmach="lb")
                    tc_ = emit("v", lambda e: e.tensor_copy(out=lbb2[:], in_=lb32[:]),
                               (tl,))
                    qsl = ot % 4
                    ttr = emit("t", lambda e, qsl=qsl: e.transpose(
                        pTc[0:32, qsl * 128:(qsl + 1) * 128], lbb2[:], ident[:]),
                        (tc_,) + ((pTc_rd,) if qsl == 0 else ()))
                    lb_rd = ttr
                    if qsl == 3:
                        ot0 = ot - 3
                        dst = lbT[:, l * 1024 + ot0 * 128: l * 1024 + (ot0 + 4) * 128]
                        tev = emit("s", lambda e, dst=dst: e.copy(
                            out=dst, in_=pTc[0:32, 0:512]), (ttr,))
                        evs.append(tev)
                        pTc_rd = tev
                lbT_tok[l] = evs[-1]
            snap54 = {e: len(ops[e]) for e in STREAMS}
            bi_tok = [None] * NL
            br_rd = B_done
            for l in range(NL):
                tl = emit("q", lambda e, l=l: e.dma_start(
                    out=dqf[0:1, :], in_=bi_h[l]), (br_rd, dqf_rd), dmach="bi")
                tc_ = emit("v", lambda e, l=l: e.tensor_copy(
                    out=brow[0:1, l * 1024:(l + 1) * 1024], in_=dqf[0:1, :]), (tl,))
                br_rd = tc_
                dqf_rd = tc_
                bi_tok[l] = tc_

            snap50 = {e: len(ops[e]) for e in STREAMS}
            hTb_rd = B_done
            hTf_rd = B_done
            g_rd = B_done
            gT_rd = B_done
            z_rd = B_done
            raT_rd = [B_done, B_done]
            pY_rd = [B_done, B_done]
            pA_rd = B_done
            pZ_rd = [B_done, B_done]
            pH_rd = B_done
            gmb_rd = [B_done]
            btb_rd = [B_done]
            x2r_rd = [B_done]
            out_toks = []
            snap6l = []
            for c in range(NC5):
                t0 = c * 512
                hload = []
                for j in range(NJ):
                    dst = hTb[:, j * 512:(j + 1) * 512]
                    tl = emit("q", lambda e, j=j, t0=t0, dst=dst: e.dma_start(
                        out=dst, in_=hTs[j * 128:(j + 1) * 128, t0:t0 + 512]),
                        hT_spill_all + (hTb_rd if j == 0 else None,), dmach="hb")
                    hload.append(tl)
                twd = emit("s", lambda e: e.copy(out=hTf[:], in_=hTb[:]),
                           tuple(hload) + (hTf_rd,))
                hTb_rd = twd
                h_new_tok = twd
                hb_tok = tuple(hload)
                for blk in range(3):
                    l1, l2 = 2 * blk, 2 * blk + 1
                    tgm = emit("q", lambda e, blk=blk: e.dma_start(
                        out=gmb[:], in_=gm_h[blk].broadcast_to([128, 1024])),
                        (gmb_rd[0],), dmach="gb")
                    tbt = emit("q", lambda e, blk=blk: e.dma_start(
                        out=btb[:], in_=bt_h[blk].broadcast_to([128, 1024])),
                        (btb_rd[0],), dmach="gb")
                    tgm = tbt
                    tmm = None
                    for j in range(NJ):
                        tmm = emit("t", lambda e, j=j, l1=l1: e.matmul(
                            pA_ap, laT[:, l1 * 256 + j * 32: l1 * 256 + (j + 1) * 32],
                            hTb[:, j * 512:(j + 1) * 512],
                            start=(j == 0), stop=(j == NJ - 1)),
                            (laT_tok[l1], pA_rd) + hb_tok if j == 0 else ())
                    tra = emit("s", lambda e: e.copy(out=raT[:, 0:512], in_=pA_ap),
                               (tmm, raT_rd[0]))
                    pA_rd = tra
                    gelu_toks = []
                    for ssi in range(4):
                        pY = pYs[ssi % 2]
                        dep0 = (wT_tok[l1], tra, bi_tok[l1], t_ones1,
                                pY_rd[ssi % 2]) + hb_tok
                        if ssi % 2 == 1:
                            dep0 = dep0 + tuple(pZ_rd)
                        tmm = None
                        for oc in range(2):
                            for j in range(NJ):
                                tmm = emit("t", lambda e, j=j, oc=oc, ssi=ssi, l1=l1,
                                           pY=pY: e.matmul(
                                    pY[:, oc * 512:(oc + 1) * 512],
                                    hTb[:, j * 512 + ssi * 128: j * 512 + (ssi + 1) * 128],
                                    wT[:, l1 * NJ * 1024 + j * 1024 + oc * 512:
                                       l1 * NJ * 1024 + j * 1024 + (oc + 1) * 512],
                                    start=(j == 0), stop=False),
                                    dep0 if (oc == 0 and j == 0) else ())
                            tmm = emit("t", lambda e, oc=oc, ssi=ssi, l1=l1, pY=pY:
                                       e.matmul(
                                pY[:, oc * 512:(oc + 1) * 512],
                                raT[0:32, ssi * 128:(ssi + 1) * 128],
                                lbT[:, l1 * 1024 + oc * 512: l1 * 1024 + (oc + 1) * 512],
                                start=False, stop=False), ())
                            tmm = emit("t", lambda e, oc=oc, l1=l1, pY=pY: e.matmul(
                                pY[:, oc * 512:(oc + 1) * 512],
                                ones1[0:1, 0:128],
                                brow[0:1, l1 * 1024 + oc * 512: l1 * 1024 + (oc + 1) * 512],
                                start=False, stop=True), ())
                        m_ap, rs_ap, nm_ap, tm, trs_, tnm, reg, tsq = ln_stats(
                            pY[:], 1024, (tmm,))
                        ts1 = emit("v", lambda e, m_ap=m_ap, pY=pY: e.scalar_tensor_tensor(
                            out=zbuf[:], in0=pY[:], scalar=m_ap,
                            in1=gmb[:], op0=ALU.subtract, op1=ALU.mult),
                            (tmm, tgm, z_rd, tsq), strict=(tm,))
                        gmb_rd[0] = ts1
                        reg(ts1)
                        ts2 = emit("v", lambda e, rs_ap=rs_ap: e.scalar_tensor_tensor(
                            out=zbuf[:], in0=zbuf[:], scalar=rs_ap,
                            in1=btb[:], op0=ALU.mult, op1=ALU.add),
                            (tbt, ts1), strict=(trs_,))
                        btb_rd[0] = ts2
                        reg(ts2)
                        pY_rd[ssi % 2] = ts2
                        tg = emit("s", lambda e, ssi=ssi: e.activation(
                            out=g_sb[:, ssi * 1024:(ssi + 1) * 1024], in_=zbuf[:],
                            func=AF.Gelu), (ts2, g_rd if ssi == 0 else None))
                        z_rd = tg
                        gelu_toks.append(tg)
                    gt_evs = []
                    for j in range(NJ):
                        ttr = None
                        for ssi in range(4):
                            ttr = emit("t", lambda e, j=j, ssi=ssi: e.transpose(
                                pTc[:, ssi * 128:(ssi + 1) * 128],
                                g_sb[:, ssi * 1024 + j * 128: ssi * 1024 + (j + 1) * 128],
                                ident[:]),
                                (gelu_toks[ssi],) + ((pTc_rd, gT_rd) if (ssi == 0) else ()))
                        tev = emit("s", lambda e, j=j: e.copy(
                            out=gT[:, j * 512:(j + 1) * 512], in_=pTc[:, 0:512]), (ttr,))
                        pTc_rd = tev
                        gt_evs.append(tev)
                    g_rd = gt_evs[-1]
                    tmm = None
                    for j in range(NJ):
                        tmm = emit("t", lambda e, j=j, l2=l2: e.matmul(
                            pA_ap, laT[:, l2 * 256 + j * 32: l2 * 256 + (j + 1) * 32],
                            gT[:, j * 512:(j + 1) * 512],
                            start=(j == 0), stop=(j == NJ - 1)),
                            ((laT_tok[l2], pA_rd) + tuple(gt_evs)) if j == 0 else ())
                    tra2 = emit("s", lambda e: e.copy(out=raT[:, 512:1024], in_=pA_ap),
                                (tmm, raT_rd[1]))
                    pA_rd = tra2
                    hupd_toks = []
                    hb_new = []
                    for oo in range(NJ):
                        zsl = oo % 2
                        dep0 = (wT_tok[l2], tra2, bi_tok[l2], pZ_rd[zsl],
                                pY_rd[1]) + tuple(gt_evs)
                        tmm = None
                        for j in range(NJ):
                            tmm = emit("t", lambda e, j=j, oo=oo, l2=l2, zsl=zsl: e.matmul(
                                pZ[:, zsl * 512:(zsl + 1) * 512],
                                wT[:, l2 * NJ * 1024 + j * 1024 + oo * 128:
                                   l2 * NJ * 1024 + j * 1024 + (oo + 1) * 128],
                                gT[:, j * 512:(j + 1) * 512],
                                start=(j == 0), stop=False),
                                dep0 if j == 0 else ())
                        tmm = emit("t", lambda e, oo=oo, l2=l2, zsl=zsl: e.matmul(
                            pZ[:, zsl * 512:(zsl + 1) * 512],
                            lbT[:, l2 * 1024 + oo * 128: l2 * 1024 + (oo + 1) * 128],
                            raT[0:32, 512:1024],
                            start=False, stop=False), ())
                        tmm = emit("t", lambda e, oo=oo, l2=l2, zsl=zsl: e.matmul(
                            pZ[:, zsl * 512:(zsl + 1) * 512],
                            brow[0:1, l2 * 1024 + oo * 128: l2 * 1024 + (oo + 1) * 128],
                            ones1[0:1, 0:512],
                            start=False, stop=True), ())
                        thu = emit("v", lambda e, oo=oo, zsl=zsl: e.tensor_tensor(
                            out=hTf[:, oo * 512:(oo + 1) * 512],
                            in0=pZ[:, zsl * 512:(zsl + 1) * 512],
                            in1=hTf[:, oo * 512:(oo + 1) * 512], op=ALU.add),
                            (tmm, h_new_tok))
                        pZ_rd[zsl] = thu
                        hupd_toks.append(thu)
                        if blk < 2:
                            thb = emit("s", lambda e, oo=oo: e.copy(
                                out=hTb[:, oo * 512:(oo + 1) * 512],
                                in_=hTf[:, oo * 512:(oo + 1) * 512]), (thu, hTb_rd))
                            hb_new.append(thb)
                    raT_rd[0] = tmm
                    raT_rd[1] = tmm
                    h_new_tok = hupd_toks[-1]
                    if blk < 2:
                        hb_tok = tuple(hb_new)
                        hTb_rd = hb_new[-1]
                    else:
                        hTb_rd = h_new_tok
                    hTf_rd = h_new_tok
                for ssi in range(4):
                    osl = ssi % 2
                    row0 = t0 + ssi * 128
                    xa = x2res[:]
                    tlx = emit("q", lambda e, row0=row0, xa=xa: e.dma_start(
                        out=xa, in_=x2s[row0:row0 + 128, :]),
                        x2_spill_all + (x2r_rd[0],), dmach="x2r")
                    ttr = None
                    for j in range(NJ):
                        ttr = emit("t", lambda e, j=j, ssi=ssi: e.transpose(
                            pH[:, j * 128:(j + 1) * 128],
                            hTf[:, j * 512 + ssi * 128: j * 512 + (ssi + 1) * 128],
                            identf[:]),
                            tuple(hupd_toks) + ((pH_rd,) if j == 0 else ()))
                    m_ap, rs_ap, nm_ap, tm, trs_, tnm, reg, tsq = ln_stats(
                        pH[:], 1024, (ttr,))
                    tap = emit("s", lambda e, rs_ap=rs_ap, nm_ap=nm_ap: e.activation(
                        out=zbuf[:], in_=pH[:], func=AF.Identity,
                        scale=rs_ap, bias=nm_ap),
                        (tnm, ttr, z_rd), strict=(trs_,))
                    reg(tap)
                    pH_rd = tap
                    hTf_rd = tap
                    tadd = emit("v", lambda e, xa=xa: e.tensor_add(
                        out=xa, in0=zbuf[:], in1=xa), (tap, tlx))
                    z_rd = tadd
                    tout = emit("g", lambda e, row0=row0, xa=xa: e.dma_start(
                        out=out_h[row0:row0 + 128, :], in_=xa), (tadd,), dmach="go")
                    x2r_rd[0] = tout
                    out_toks.append(tout)
                snap6l.append({e: len(ops[e]) for e in STREAMS})
            waitop("g", (out_toks[-1],))

            if PH != 3:
                if PH >= 60:
                    snap = snap6l[PH - 60]
                elif PH >= 50:
                    snap = {50: snap50, 51: snap51, 52: snap52,
                            53: snap53, 54: snap54}[PH]
                elif PH >= 40:
                    snap = {40: snap40, 41: snap41, 42: snap42, 43: snap43,
                            44: snap44, 45: snap45, 46: snap46}[PH]
                    if PH == 40:
                        snap = dict(snap40)
                        snap["v"] += int(os.environ.get("KDV", "0"))
                        snap["s"] += int(os.environ.get("KDS", "0"))
                elif PH >= 30:
                    snap = {30: snap30, 31: snap31, 32: snap32, 33: snap33,
                            34: snap34, 35: snap35}[PH]
                elif PH >= 20:
                    snap = snap2l[PH - 20]
                else:
                    snap = {10: snap10, 11: snap11, 12: snap12, 13: snap13,
                            1: snap1, 2: snap2}[PH]
                for e in STREAMS:
                    del ops[e][snap[e]:]
                ngo = sum(1 for o in ops["g"] if o[2] == "go")
                ops["g"].append(((), (), "go", lambda e: e.dma_start(
                    out=out_h[0:128, 0:256], in_=stp[:]), 16))
                ops["g"].append(((("go", (ngo + 1) * 16),), (), None, None, 0))

            def run_stream(name, eng):
                hi = {}
                for ds, st, ch, fn, inc in ops[name]:
                    need = {}
                    for (pe, val) in ds + st:
                        if val > need.get(pe, 0):
                            need[pe] = val
                    for pe, val in need.items():
                        if hi.get(pe, 0) >= val:
                            continue
                        eng.wait_ge(sems[pe], val)
                        hi[pe] = val
                    if fn is not None:
                        fn(eng).then_inc(sems[ch], inc)

            with nc.Block() as blk:
                @blk.tensor
                def _(eng):
                    run_stream("t", eng)

                @blk.vector
                def _(eng):
                    run_stream("v", eng)

                @blk.scalar
                def _(eng):
                    run_stream("s", eng)

                @blk.gpsimd
                def _(eng):
                    run_stream("g", eng)

                @blk.sync
                def _(eng):
                    run_stream("q", eng)
    return nc


def _consts():
    ident = np.eye(128, dtype=np.float32)
    p = np.arange(128).reshape(128, 1)
    f = np.arange(384).reshape(1, 384)
    cmask = (f >= p + 128).astype(np.float32)
    return (ident.astype(ml_dtypes.bfloat16), ident,
            cmask.astype(ml_dtypes.bfloat16))


def kernel(L0_attn_x, L0_attn_Wq, L0_attn_Wk, L0_attn_Wv, L0_attn_Wo,
           L0_ffn_qweights, L0_ffn_scales, L0_ffn_biases, L0_ffn_lora_a,
           L0_ffn_lora_b, L0_ffn_ln_gamma, L0_ffn_ln_beta):
    B, S, Dm = L0_attn_x.shape
    assert (B, S, Dm) == (8, 4096, 1024)
    idb, idf, cm = _consts()
    nc = build(S)
    common = dict(
        wq=np.ascontiguousarray(L0_attn_Wq, np.float32),
        wk=np.ascontiguousarray(L0_attn_Wk, np.float32),
        wv=np.ascontiguousarray(L0_attn_Wv, np.float32),
        wo=np.ascontiguousarray(L0_attn_Wo, np.float32),
        qw=np.ascontiguousarray(L0_ffn_qweights, np.int32),
        sc=np.ascontiguousarray(L0_ffn_scales, np.float32),
        bi=np.ascontiguousarray(L0_ffn_biases, np.float32).reshape(6, 1, 1024),
        la=np.ascontiguousarray(L0_ffn_lora_a, np.float32),
        lb=np.ascontiguousarray(L0_ffn_lora_b, np.float32),
        gm=np.ascontiguousarray(L0_ffn_ln_gamma, np.float32).reshape(3, 1, 1024),
        bt=np.ascontiguousarray(L0_ffn_ln_beta, np.float32).reshape(3, 1, 1024),
        idb=idb, idf=idf, cm=cm,
    )
    in_maps = [dict(common, x=np.ascontiguousarray(L0_attn_x[b], np.float32))
               for b in range(8)]
    res = run_bass_kernel_spmd(nc, in_maps, core_ids=list(range(8)))
    return np.stack([np.asarray(r["out"], np.float32) for r in res.results], axis=0)


# version 06
# speedup vs baseline: 1.0013x; 1.0013x over previous
import os
import numpy as np
import ml_dtypes
import concourse.bass as bass
import concourse.mybir as mybir
from concourse.bass_utils import run_bass_kernel_spmd
from contextlib import ExitStack

F32 = mybir.dt.float32
BF16 = mybir.dt.bfloat16
I32 = mybir.dt.int32
AF = mybir.ActivationFunctionType
ALU = mybir.AluOpType
AX = mybir.AxisListType

D = 1024
NJ = 8
R = 32
NL = 6
EPS = 1e-5
ISQ2 = float(1.0 / np.sqrt(2.0))


def build(S):
    PH = int(os.environ.get("KPH", "3"))
    assert S % 512 == 0
    NT = S // 128
    NC2 = S // 256
    NC5 = S // 512
    nc = bass.Bass("TRN2", target_bir_lowering=False)

    x_h = nc.dram_tensor("x", [S, D], F32, kind="ExternalInput").ap()
    wq_h = nc.dram_tensor("wq", [D, D], F32, kind="ExternalInput").ap()
    wk_h = nc.dram_tensor("wk", [D, D], F32, kind="ExternalInput").ap()
    wv_h = nc.dram_tensor("wv", [D, D], F32, kind="ExternalInput").ap()
    wo_h = nc.dram_tensor("wo", [D, D], F32, kind="ExternalInput").ap()
    qw_h = nc.dram_tensor("qw", [NL, D, D], I32, kind="ExternalInput").ap()
    sc_h = nc.dram_tensor("sc", [NL, D, 64], F32, kind="ExternalInput").ap()
    bi_h = nc.dram_tensor("bi", [NL, 1, D], F32, kind="ExternalInput").ap()
    la_h = nc.dram_tensor("la", [NL, R, D], F32, kind="ExternalInput").ap()
    lb_h = nc.dram_tensor("lb", [NL, D, R], F32, kind="ExternalInput").ap()
    gm_h = nc.dram_tensor("gm", [3, 1, D], F32, kind="ExternalInput").ap()
    bt_h = nc.dram_tensor("bt", [3, 1, D], F32, kind="ExternalInput").ap()
    idb_h = nc.dram_tensor("idb", [128, 128], BF16, kind="ExternalInput").ap()
    idf_h = nc.dram_tensor("idf", [128, 128], F32, kind="ExternalInput").ap()
    cm_h = nc.dram_tensor("cm", [128, 384], BF16, kind="ExternalInput").ap()
    out_h = nc.dram_tensor("out", [S, D], F32, kind="ExternalOutput").ap()
    qTs = nc.dram_tensor("qTs", [D, S], BF16, kind="Internal").ap()
    vs_h = nc.dram_tensor("vs", [S, D], BF16, kind="Internal").ap()
    x2s = nc.dram_tensor("x2s", [S, D], F32, kind="Internal").ap()
    hTs = nc.dram_tensor("hTs", [D, S], BF16, kind="Internal").ap()

    STREAMS = ("t", "v", "s", "g", "q")
    DMACH = ("qc", "xt0", "xt1", "wf0", "wf1", "wo0", "wo1", "qb0", "qb1",
             "xr0", "xr1", "gsv0", "gsv1", "gsq0", "gsq1",
             "gv0", "gv1", "gv2", "gv3", "go",
             "sd0", "sd1", "hs0", "hs1", "qw", "sc2", "lo", "lb", "bi", "gb", "hb", "x2r")
    ops = {e: [] for e in STREAMS}
    cnt = {e: 0 for e in STREAMS + DMACH}

    def emit(stream, fn, deps=(), strict=(), dmach=None):
        ch = dmach or stream
        inc = 16 if dmach else 1
        cnt[ch] += 1
        tok = (ch, cnt[ch] * inc)
        ds = tuple(d for d in deps
                   if d is not None and (d[0] != ch or ch in ("t", "v", "s")))
        st = tuple(d for d in strict if d is not None)
        ops[stream].append((ds, st, ch, fn, inc))
        return tok

    def waitop(stream, deps):
        ops[stream].append((tuple(d for d in deps if d is not None), (), None, None, 0))

    NSLOT = 32
    slot_read_toks = [[] for _ in range(NSLOT)]
    slot_state = {"i": 0}

    with ExitStack() as OUTER:
        EO = OUTER.enter_context
        sems = {ch: EO(nc.semaphore("sem_" + ch)) for ch in cnt}
        ident = EO(nc.sbuf_tensor("ident", [128, 128], BF16))
        identf = EO(nc.sbuf_tensor("identf", [128, 128], F32))
        cmask = EO(nc.sbuf_tensor("cmask", [128, 384], BF16))
        stp = EO(nc.sbuf_tensor("stp", [128, NSLOT * 8], F32))
        sqd = EO(nc.sbuf_tensor("sqd", [128, 1024], BF16))
        ones1 = EO(nc.sbuf_tensor("ones1", [1, 512], BF16))
        onesc = EO(nc.sbuf_tensor("onesc", [128, 1], BF16))
        brow = EO(nc.sbuf_tensor("brow", [1, NL * 1024], BF16))
        laT = EO(nc.sbuf_tensor("laT", [128, NL * 256], BF16))
        lbT = EO(nc.sbuf_tensor("lbT", [32, NL * 1024], BF16))
        rsrow = EO(nc.sbuf_tensor("rsrow", [1, 256], F32))
        rs_col = EO(nc.sbuf_tensor("rs_col", [128, 2], F32))
        PB0 = EO(nc.psum_tensor("PB0", [128, 1024], F32))
        PB1 = EO(nc.psum_tensor("PB1", [128, 1024], F32))
        PB2 = EO(nc.psum_tensor("PB2", [128, 1024], F32))
        PB3 = EO(nc.psum_tensor("PB3", [128, 1024], BF16))
        PB4 = EO(nc.psum_tensor("PB4", [128, 512], F32))

        emit("q", lambda e: e.dma_start(out=ident[:], in_=idb_h), dmach="qc")
        emit("q", lambda e: e.dma_start(out=identf[:], in_=idf_h), dmach="qc")
        t_id = t_idf = t_cm = emit(
            "q", lambda e: e.dma_start(out=cmask[:], in_=cm_h), dmach="qc")
        t_ones1 = emit("v", lambda e: e.memset(ones1[:], 1.0))
        t_onesc = emit("v", lambda e: e.memset(onesc[:], 1.0))

        def ln_stats(src_ap, W, deps):
            k = slot_state["i"] % NSLOT
            slot_state["i"] += 1
            war = tuple(slot_read_toks[k])
            slot_read_toks[k] = []
            c0 = k * 8
            m_ap = stp[:, c0:c0 + 1]
            rs_ap = stp[:, c0 + 1:c0 + 2]
            nm_ap = stp[:, c0 + 2:c0 + 3]
            sq_ap = stp[:, c0 + 3:c0 + 4]
            m2_ap = stp[:, c0 + 4:c0 + 5]
            deps = tuple(deps)
            t1 = emit("v", lambda e: e.reduce_sum(out=m_ap, in_=src_ap, axis=AX.X),
                      deps + war)
            t2 = emit("s", lambda e: e.activation(out=sqd[:, 0:W], in_=src_ap,
                                                  func=AF.Square, accum_out=sq_ap),
                      deps + war + (t1,))
            t3 = emit("v", lambda e: e.tensor_scalar_mul(out=m_ap, in0=m_ap,
                                                         scalar1=1.0 / W),
                      strict=(t1,))
            t4 = emit("s", lambda e: e.square(out=m2_ap, in_=m_ap), (t3,), strict=(t2,))
            t5 = emit("v", lambda e: e.scalar_tensor_tensor(
                out=rs_ap, in0=sq_ap, scalar=1.0 / W, in1=m2_ap,
                op0=ALU.mult, op1=ALU.subtract), (t2, t4))
            t5b = emit("v", lambda e: e.tensor_scalar_add(out=rs_ap, in0=rs_ap,
                                                          scalar1=EPS),
                       strict=(t5,))
            t6 = emit("s", lambda e: e.activation(out=rs_ap, in_=rs_ap,
                                                  func=AF.Sqrt), (t5b,))
            t7 = emit("v", lambda e: e.reciprocal(out=rs_ap, in_=rs_ap), (t6,))
            t8 = emit("v", lambda e: e.scalar_tensor_tensor(
                out=nm_ap, in0=m_ap, scalar=-1.0, in1=rs_ap,
                op0=ALU.mult, op1=ALU.mult), (t3,), strict=(t7,))

            def reg(tok):
                slot_read_toks[k].append(tok)
            reg(t8)
            return m_ap, rs_ap, nm_ap, t3, t7, t8, reg, t2

        with ExitStack() as AB:
            EAB = AB.enter_context
            kT = EAB(nc.sbuf_tensor("kT", [128, NJ * S], BF16))
            kT_done = [None]

            with ExitStack() as A:
                EA = A.enter_context
                xnT = EA(nc.sbuf_tensor("xnT", [128, NJ * S], BF16))
                wf32 = EA(nc.sbuf_tensor("wf32", [128, 2 * 1024], F32))
                wbuf = EA(nc.sbuf_tensor("wbuf", [128, NJ * 1024], BF16))
                xtile = EA(nc.sbuf_tensor("xtile", [128, 2 * 1024], F32))
                xn_sb = EA(nc.sbuf_tensor("xn_sb", [128, 2 * 1024], BF16))
                vst = EA(nc.sbuf_tensor("vst", [128, 2 * 1024], BF16))
                qst = EA(nc.sbuf_tensor("qst", [128, 2 * 512], BF16))
                pP = PB0
                pTa = PB3

                xt_rd = [None, None]
                xn_rd = [None, None]
                pTa_rd = [None]
                xnT_tok = {}
                for tt in range(NT):
                    sl = tt % 2
                    xa = xtile[:, sl * 1024:(sl + 1) * 1024]
                    tld = emit("q", lambda e, tt=tt, xa=xa: e.dma_start(
                        out=xa, in_=x_h[tt * 128:(tt + 1) * 128, :]),
                        deps=(xt_rd[sl],), dmach="xt%d" % sl)
                    m_ap, rs_ap, nm_ap, tm, trs, tnm, reg, tsq = ln_stats(xa, 1024, (tld,))
                    na = xn_sb[:, sl * 1024:(sl + 1) * 1024]
                    tap = emit("s", lambda e, xa=xa, na=na, rs_ap=rs_ap, nm_ap=nm_ap:
                               e.activation(out=na, in_=xa, func=AF.Identity,
                                            scale=rs_ap, bias=nm_ap),
                               (tnm, xn_rd[sl]), strict=(trs,))
                    reg(tap)
                    xt_rd[sl] = tap
                    last_ev = None
                    for j in range(NJ):
                        psl = j % 2
                        pa = pTa[:, psl * 128:(psl + 1) * 128]
                        ttr = emit("t", lambda e, na=na, j=j, pa=pa: e.transpose(
                            pa, na[:, j * 128:(j + 1) * 128], ident[:]),
                            (tap, t_id, pTa_rd[0]))
                        dst = xnT[:, j * S + tt * 128: j * S + (tt + 1) * 128]
                        tev = emit("s", lambda e, pa=pa, dst=dst: e.copy(out=dst, in_=pa),
                                   (ttr,))
                        pTa_rd[0] = tev
                        xnT_tok[(j, tt)] = tev
                        last_ev = tev
                    xn_rd[sl] = last_ev
                    if tt == 0:
                        snap10 = {e: len(ops[e]) for e in STREAMS}
                xnT_all = last_ev

                def xnT_ready(tc_lo, tc_hi):
                    return xnT_tok[(NJ - 1, tc_hi - 1)]


                snap11 = {e: len(ops[e]) for e in STREAMS}
                wf_rd = [None, None]
                def load_wproj(w_hbm, deps):
                    toks = []
                    for j in range(NJ):
                        sl = j % 2
                        wa = wf32[:, sl * 1024:(sl + 1) * 1024]
                        tl = emit("q", lambda e, wa=wa, j=j, w_hbm=w_hbm: e.dma_start(
                            out=wa, in_=w_hbm[j * 128:(j + 1) * 128, :]),
                            deps=(wf_rd[sl],) + tuple(deps), dmach="wf%d" % sl)
                        tc = emit("v", lambda e, wa=wa, j=j: e.tensor_copy(
                            out=wbuf[:, j * 1024:(j + 1) * 1024], in_=wa), (tl,))
                        wf_rd[sl] = tc
                        toks.append(tc)
                    return toks

                wk_toks = load_wproj(wk_h, ())
                spill_fin = {}
                pP_rd = [None, None]
                kT_tok = {}
                for m in range(NJ):
                    for tc in range(S // 512):
                        psl = (m * (S // 512) + tc) % 2
                        pa = pP[:, psl * 512:(psl + 1) * 512]
                        dep0 = (xnT_tok[(NJ - 1, tc * 4 + 3)],
                                pP_rd[psl]) + tuple(wk_toks)
                        tmm = None
                        for j in range(NJ):
                            tmm = emit("t", lambda e, j=j, m=m, tc=tc, pa=pa: e.matmul(
                                pa,
                                wbuf[:, j * 1024 + m * 128: j * 1024 + (m + 1) * 128],
                                xnT[:, j * S + tc * 512: j * S + (tc + 1) * 512],
                                start=(j == 0), stop=(j == NJ - 1)),
                                dep0 if j == 0 else ())
                        dst = kT[:, m * S + tc * 512: m * S + (tc + 1) * 512]
                        tev = emit("s", lambda e, pa=pa, dst=dst: e.copy(out=dst, in_=pa),
                                   (tmm,))
                        pP_rd[psl] = tev
                        kT_tok[(m, tc)] = tev
                kT_done[0] = tev

                snap12 = {e: len(ops[e]) for e in STREAMS}
                wv_toks = load_wproj(wv_h, (kT_done[0],))
                vst_rd = [None, None]
                for tt in range(NT):
                    sl = tt % 2
                    for dv in range(2):
                        psl = dv
                        pa = pP[:, psl * 512:(psl + 1) * 512]
                        dep0 = (pP_rd[psl], vst_rd[sl]) + tuple(wv_toks)
                        tmm = None
                        for j in range(NJ):
                            tmm = emit("t", lambda e, j=j, tt=tt, dv=dv, pa=pa: e.matmul(
                                pa,
                                xnT[:, j * S + tt * 128: j * S + (tt + 1) * 128],
                                wbuf[:, j * 1024 + dv * 512: j * 1024 + (dv + 1) * 512],
                                start=(j == 0), stop=(j == NJ - 1)),
                                dep0 if j == 0 else ())
                        dst = vst[:, sl * 1024 + dv * 512: sl * 1024 + (dv + 1) * 512]
                        tev = emit("s", lambda e, pa=pa, dst=dst: e.copy(out=dst, in_=pa),
                                   (tmm,))
                        pP_rd[psl] = tev
                        if dv == 0:
                            tev0 = tev
                    tsp = emit("g", lambda e, tt=tt, sl=sl: e.dma_start(
                        out=vs_h[tt * 128:(tt + 1) * 128, :],
                        in_=vst[:, sl * 1024:(sl + 1) * 1024]), (tev0, tev),
                        dmach="gsv%d" % sl)
                    vst_rd[sl] = tsp
                    spill_fin[tsp[0]] = tsp
                v_done = tev

                snap13 = {e: len(ops[e]) for e in STREAMS}
                wq_toks = load_wproj(wq_h, (v_done,))
                qst_rd = [None, None]
                for m in range(NJ):
                    for tc in range(S // 512):
                        psl = (m * (S // 512) + tc) % 2
                        sl = (m * (S // 512) + tc) % 2
                        pa = pP[:, psl * 512:(psl + 1) * 512]
                        dep0 = (pP_rd[psl], qst_rd[sl]) + tuple(wq_toks)
                        tmm = None
                        for j in range(NJ):
                            tmm = emit("t", lambda e, j=j, m=m, tc=tc, pa=pa: e.matmul(
                                pa,
                                wbuf[:, j * 1024 + m * 128: j * 1024 + (m + 1) * 128],
                                xnT[:, j * S + tc * 512: j * S + (tc + 1) * 512],
                                start=(j == 0), stop=(j == NJ - 1)),
                                dep0 if j == 0 else ())
                        dst = qst[:, sl * 512:(sl + 1) * 512]
                        tev = emit("s", lambda e, pa=pa, dst=dst: e.copy(out=dst, in_=pa),
                                   (tmm,))
                        pP_rd[psl] = tev
                        tsp = emit("g", lambda e, m=m, tc=tc, dst=dst: e.dma_start(
                            out=qTs[m * 128:(m + 1) * 128, tc * 512:(tc + 1) * 512],
                            in_=dst), (tev,), dmach="gsq%d" % sl)
                        qst_rd[sl] = tsp
                        spill_fin[tsp[0]] = tsp
                spill_final = tuple(spill_fin.values())
            snap1 = {e: len(ops[e]) for e in STREAMS}
            with ExitStack() as B:
                EB = B.enter_context
                qbuf = EB(nc.sbuf_tensor("qbuf", [128, 2 * NJ * 256], BF16))
                vbuf = EB(nc.sbuf_tensor("vbuf", [128, 4 * 1024], BF16))
                expT = EB(nc.sbuf_tensor("expT", [128, 4 * 256], BF16))
                attn_sb = EB(nc.sbuf_tensor("attn_sb", [128, 2 * 1024], BF16))
                attnT = EB(nc.sbuf_tensor("attnT", [128, NJ * 256], BF16))
                wo32 = EB(nc.sbuf_tensor("wo32", [128, 2 * 1024], F32))
                wo_sb = EB(nc.sbuf_tensor("wo_sb", [128, NJ * 1024], BF16))
                x_res = EB(nc.sbuf_tensor("x_res", [128, 2 * 1024], F32))
                x2_sb = EB(nc.sbuf_tensor("x2_sb", [128, 2 * 1024], F32))
                x2n_sb = EB(nc.sbuf_tensor("x2n_sb", [128, 2 * 1024], BF16))
                xnT_st = EB(nc.sbuf_tensor("xnT_st", [128, 2 * NJ * 256], BF16))
                lnb = EB(nc.sbuf_tensor("lnb", [128, 1024], F32))
                lnsrc = EB(nc.sbuf_tensor("lnsrc", [128, 4 * 1024], F32))
                pAV0 = PB0
                pAV1 = PB1
                pS = PB2
                pRS_ap = PB4[0:1, 256:512]
                pTb = PB3
                pTrs = PB4

                A_done = emit("v", lambda e: e.memset(stp[:, 255:256], 0.0),
                              spill_final + (kT_done[0],))

                wo_toks = []
                wo_rd2 = [A_done, A_done]
                for j in range(NJ):
                    sl = j % 2
                    wa = wo32[:, sl * 1024:(sl + 1) * 1024]
                    tl = emit("q", lambda e, wa=wa, j=j: e.dma_start(
                        out=wa, in_=wo_h[j * 128:(j + 1) * 128, :]),
                        deps=(wo_rd2[sl],), dmach="wo%d" % sl)
                    tc_ = emit("v", lambda e, wa=wa, j=j: e.tensor_copy(
                        out=wo_sb[:, j * 1024:(j + 1) * 1024], in_=wa), (tl,))
                    wo_rd2[sl] = tc_
                    wo_toks.append(tc_)

                pAVs = [pAV0, pAV1]
                qb_rd = [A_done, A_done]
                vb_rd = [A_done] * 4
                exp_rd = [A_done] * 4
                pS_rd = [A_done, A_done]
                pAV_rd = [A_done, A_done]
                pRS_rd = A_done
                rsrow_rd = A_done
                rscol_rd = A_done
                attn_rd = [A_done, A_done]
                attnT_rd = [A_done]
                xres_rd = [A_done, A_done]
                x2_rd = [A_done, A_done]
                lnsrc_rd = [A_done] * 4
                x2sp_rd = [None, None]
                x2n_rd = [A_done, A_done]
                xst_rd = [A_done, A_done]
                lnb_rd = [A_done]
                vb_ctr = [0]
                pending_tail = None
                snap2l = []
                x2_spill_toks = []
                hT_spill_toks = []

                for c in range(NC2):
                    s0 = c * 256
                    qsl = c % 2
                    q_toks = []
                    for m in range(NJ):
                        dst = qbuf[:, qsl * NJ * 256 + m * 256: qsl * NJ * 256 + (m + 1) * 256]
                        tq = emit("q", lambda e, m=m, s0=s0, dst=dst: e.dma_start(
                            out=dst, in_=qTs[m * 128:(m + 1) * 128, s0:s0 + 256]),
                            spill_final + (qb_rd[qsl],), dmach="qb%d" % qsl)
                        q_toks.append(tq)
                    n_tt = 2 * c + 2
                    t_score_last = None
                    t_rs_last = None
                    t_av_last = None
                    pend = None
                    for tt in range(n_tt + 1):
                        if tt < n_tt:
                            vsl = vb_ctr[0] % 4
                            vb_ctr[0] += 1
                            va = vbuf[:, vsl * 1024:(vsl + 1) * 1024]
                            tv = emit("g", lambda e, tt=tt, va=va: e.dma_start(
                                out=va, in_=vs_h[tt * 128:(tt + 1) * 128, :]),
                                spill_final + (vb_rd[vsl],), dmach="gv%d" % vsl)
                            psl = tt % 2
                            pa = pS[:, psl * 512: psl * 512 + 256]
                            dep0 = tuple(q_toks) + (pS_rd[psl], kT_done[0])
                            tmm = None
                            for m in range(NJ):
                                tmm = emit("t", lambda e, m=m, tt=tt, c=c, pa=pa,
                                           qsl=qsl: e.matmul(
                                    pa,
                                    kT[:, m * S + tt * 128: m * S + (tt + 1) * 128],
                                    qbuf[:, qsl * NJ * 256 + m * 256: qsl * NJ * 256 + (m + 1) * 256],
                                    start=(m == 0), stop=(m == NJ - 1)),
                                    dep0 if m == 0 else ())
                            t_score_last = tmm
                            esl = tt % 4
                            ea = expT[:, esl * 256:(esl + 1) * 256]
                            tex = emit("s", lambda e, pa=pa, ea=ea: e.activation(
                                out=ea, in_=pa, func=AF.Exp, scale=0.03125),
                                (tmm, exp_rd[esl]))
                            pS_rd[psl] = tex
                            tmask = None
                            if tt >= 2 * c:
                                kk = tt - 2 * c
                                cs = 128 - 128 * kk
                                tmask = emit("v", lambda e, ea=ea, cs=cs:
                                             e.tensor_tensor(
                                    out=ea, in0=ea, in1=cmask[:, cs:cs + 256],
                                    op=ALU.mult), (tex, t_cm))
                            cur = (tt, ea, va, tmask or tex, tv, vsl, esl)
                        else:
                            cur = None
                        if pend is not None:
                            ptt, ea, va, tconsume, tv, vsl, esl = pend
                            trs = emit("t", lambda e, ea=ea, ptt=ptt, n_tt=n_tt:
                                       e.matmul(
                                pRS_ap, onesc[:], ea,
                                start=(ptt == 0), stop=(ptt == n_tt - 1)),
                                (tconsume, t_onesc) + ((pRS_rd,) if ptt == 0 else ()))
                            t_rs_last = trs
                            tav = None
                            for ss in range(2):
                                for dv in range(2):
                                    dep = (tconsume, tv)
                                    if ptt == 0:
                                        dep = dep + (pAV_rd[ss],)
                                    tav = emit("t", lambda e, ea=ea, ss=ss, dv=dv,
                                               va=va, ptt=ptt, n_tt=n_tt: e.matmul(
                                        pAVs[ss][:, dv * 512:(dv + 1) * 512],
                                        ea[:, ss * 128:(ss + 1) * 128],
                                        va[:, dv * 512:(dv + 1) * 512],
                                        start=(ptt == 0), stop=(ptt == n_tt - 1)),
                                        dep)
                            t_av_last = tav
                            vb_rd[vsl] = tav
                            exp_rd[esl] = tav
                        pend = cur
                    qb_rd[qsl] = t_score_last
                    if c == 0:
                        snap30 = {e: len(ops[e]) for e in STREAMS}
                    trc = emit("v", lambda e: e.reciprocal(out=rsrow[:], in_=pRS_ap),
                               (t_rs_last, rsrow_rd))
                    ttr0 = emit("t", lambda e: e.transpose(
                        pTrs[:, 0:1], rsrow[0:1, 0:128], identf[0:1, 0:1]),
                        (trc, t_idf, rscol_rd))
                    ttr1 = emit("t", lambda e: e.transpose(
                        pTrs[:, 1:2], rsrow[0:1, 128:256], identf[0:1, 0:1]), (trc,))
                    tec = emit("s", lambda e: e.copy(out=rs_col[:], in_=pTrs[:, 0:2]),
                               (ttr0, ttr1))
                    rsrow_rd = ttr1
                    pRS_rd = tec
                    if c == 0:
                        snap31 = {e: len(ops[e]) for e in STREAMS}
                    tproj = [None, None]
                    for ss in range(2):
                        aa = attn_sb[:, ss * 1024:(ss + 1) * 1024]
                        tn = emit("s", lambda e, ss=ss, aa=aa: e.activation(
                            out=aa, in_=pAVs[ss][:], func=AF.Copy,
                            scale=rs_col[:, ss:ss + 1]),
                            (t_av_last, attn_rd[ss]), strict=(tec,))
                        attn_rd[ss] = tn
                        pAV_rd[ss] = tn
                    rscol_rd = attn_rd[1]
                    if c == 0:
                        snap32 = {e: len(ops[e]) for e in STREAMS}
                    at_evs = []
                    for j in range(NJ):
                        tt0 = emit("t", lambda e, j=j: e.transpose(
                            pTb[:, 0:128],
                            attn_sb[:, 0 * 1024 + j * 128: 0 * 1024 + (j + 1) * 128],
                            ident[:]), (attn_rd[0], attnT_rd[0]))
                        tt1 = emit("t", lambda e, j=j: e.transpose(
                            pTb[:, 128:256],
                            attn_sb[:, 1 * 1024 + j * 128: 1 * 1024 + (j + 1) * 128],
                            ident[:]), (attn_rd[1],))
                        tev = emit("s", lambda e, j=j: e.copy(
                            out=attnT[:, j * 256:(j + 1) * 256], in_=pTb[:, 0:256]),
                            (tt0, tt1))
                        at_evs.append(tev)
                        attnT_rd[0] = tev
                    attn_rd[0] = at_evs[-1]
                    attn_rd[1] = at_evs[-1]
                    if c == 0:
                        snap33 = {e: len(ops[e]) for e in STREAMS}
                    for ss in range(2):
                        dep0 = tuple(at_evs) + tuple(wo_toks) + (pAV_rd[ss],)
                        tmm = None
                        for j in range(NJ):
                            for dc in range(2):
                                tmm = emit("t", lambda e, j=j, dc=dc, ss=ss: e.matmul(
                                    pAVs[ss][:, dc * 512:(dc + 1) * 512],
                                    attnT[:, j * 256 + ss * 128: j * 256 + (ss + 1) * 128],
                                    wo_sb[:, j * 1024 + dc * 512: j * 1024 + (dc + 1) * 512],
                                    start=(j == 0), stop=(j == NJ - 1)),
                                    dep0 if (j == 0 and dc == 0) else ())
                        tproj[ss] = tmm
                    if c == 0:
                        snap34 = {e: len(ops[e]) for e in STREAMS}
                    tcps = []
                    for ss in range(2):
                        lsl = 2 * (c % 2) + ss
                        la_ = lnsrc[:, lsl * 1024:(lsl + 1) * 1024]
                        tcp = emit("s", lambda e, ss=ss, la_=la_: e.copy(
                            out=la_, in_=pAVs[ss][:]),
                            (tproj[ss], lnsrc_rd[lsl]))
                        pAV_rd[ss] = tcp
                        tcps.append((tcp, la_, lsl))

                    def epilogue_tail(c, tcps):
                        s0 = c * 256
                        for ss in range(2):
                            tcp, la_, lsl = tcps[ss]
                            row0 = s0 + ss * 128
                            xsl = ss
                            xa = x_res[:, xsl * 1024:(xsl + 1) * 1024]
                            tlx = emit("q", lambda e, row0=row0, xa=xa: e.dma_start(
                                out=xa, in_=x_h[row0:row0 + 128, :]),
                                (xres_rd[xsl],), dmach="xr%d" % xsl)
                            m_ap, rs_ap, nm_ap, tm, trs_, tnm, reg, tsq = ln_stats(
                                la_, 1024, (tcp,))
                            tap = emit("s", lambda e, la_=la_, rs_ap=rs_ap,
                                       nm_ap=nm_ap:
                                       e.activation(out=lnb[:], in_=la_,
                                                    func=AF.Identity, scale=rs_ap,
                                                    bias=nm_ap),
                                       (tnm, lnb_rd[0]), strict=(trs_,))
                            reg(tap)
                            lnsrc_rd[lsl] = tap
                            x2a = x2_sb[:, ss * 1024:(ss + 1) * 1024]
                            tx2 = emit("v", lambda e, x2a=x2a, xa=xa: e.tensor_add(
                                out=x2a, in0=lnb[:], in1=xa),
                                (tap, tlx, x2_rd[ss], x2sp_rd[ss]))
                            lnb_rd[0] = tx2
                            xres_rd[xsl] = tx2
                            tspx2 = emit("s", lambda e, row0=row0, x2a=x2a:
                                         e.dma_start(
                                out=x2s[row0:row0 + 128, :], in_=x2a), (tx2,),
                                dmach="sd%d" % ss)
                            x2_spill_toks.append(tspx2)
                            x2sp_rd[ss] = tspx2
                            m2_ap, rs2_ap, nm2_ap, tm2, trs2, tnm2, reg2, _x = \
                                ln_stats(x2a, 1024, (tx2,))
                            xna = x2n_sb[:, ss * 1024:(ss + 1) * 1024]
                            txn = emit("v", lambda e, x2a=x2a, xna=xna, m2_ap=m2_ap,
                                       rs2_ap=rs2_ap: e.tensor_scalar(
                                out=xna, in0=x2a, scalar1=m2_ap, scalar2=rs2_ap,
                                op0=ALU.subtract, op1=ALU.mult),
                                (x2n_rd[ss],), strict=(tm2, trs2))
                            reg2(txn)
                            x2_rd[ss] = txn
                        stsl = c % 2
                        for j in range(NJ):
                            tt0 = emit("t", lambda e, j=j: e.transpose(
                                pTb[:, 0:128],
                                x2n_sb[:, 0 * 1024 + j * 128: (j + 1) * 128],
                                ident[:]), (x2_rd[0], attnT_rd[0]))
                            tt1 = emit("t", lambda e, j=j: e.transpose(
                                pTb[:, 128:256],
                                x2n_sb[:, 1024 + j * 128: 1024 + (j + 1) * 128],
                                ident[:]), (x2_rd[1],))
                            dst = xnT_st[:, stsl * NJ * 256 + j * 256:
                                         stsl * NJ * 256 + (j + 1) * 256]
                            tev = emit("s", lambda e, dst=dst: e.copy(
                                out=dst, in_=pTb[:, 0:256]),
                                (tt0, tt1, xst_rd[stsl] if j == 0 else None))
                            attnT_rd[0] = tev
                            tsp = emit("s", lambda e, j=j, s0=s0, dst=dst:
                                       e.dma_start(
                                out=hTs[j * 128:(j + 1) * 128, s0:s0 + 256],
                                in_=dst), (tev,), dmach="hs%d" % stsl)
                            hT_spill_toks.append(tsp)
                            if j == NJ - 1:
                                xst_rd[stsl] = tsp
                                x2n_rd[0] = tev
                                x2n_rd[1] = tev

                    if pending_tail is not None:
                        epilogue_tail(pending_tail[0], pending_tail[1])
                    pending_tail = (c, tcps)
                    snap2l.append({e: len(ops[e]) for e in STREAMS})
                epilogue_tail(pending_tail[0], pending_tail[1])
                snap2c = getattr(build, "_snap2c", None)
                x2_spill_all = tuple({t[0]: t for t in x2_spill_toks}.values())
                hT_spill_all = tuple({t[0]: t for t in hT_spill_toks}.values())
        snap2 = {e: len(ops[e]) for e in STREAMS}
        with ExitStack() as C:
            EC = C.enter_context
            wT = EC(nc.sbuf_tensor("wT", [128, NL * NJ * 1024], BF16))
            gmb = EC(nc.sbuf_tensor("gmb", [128, 1024], F32))
            btb = EC(nc.sbuf_tensor("btb", [128, 1024], F32))
            hTb = EC(nc.sbuf_tensor("hTb", [128, NJ * 512], BF16))
            hTf = EC(nc.sbuf_tensor("hTf", [128, NJ * 512], F32))
            g_sb = EC(nc.sbuf_tensor("g_sb", [128, 4 * 1024], BF16))
            gT = EC(nc.sbuf_tensor("gT", [128, NJ * 512], BF16))
            zbuf = EC(nc.sbuf_tensor("zbuf", [128, 1024], F32))
            x2res = EC(nc.sbuf_tensor("x2res", [128, 1024], F32))
            qwt = EC(nc.sbuf_tensor("qwt", [128, 1024], I32))
            dqf = EC(nc.sbuf_tensor("dqf", [128, 1024], F32))
            dqb = EC(nc.sbuf_tensor("dqb", [128, 1024], BF16))
            sct = EC(nc.sbuf_tensor("sct", [128, 64], F32))
            lab2 = EC(nc.sbuf_tensor("lab2", [32, 1024], BF16))
            lb32 = EC(nc.sbuf_tensor("lb32", [128, 32], F32))
            lbb2 = EC(nc.sbuf_tensor("lbb2", [128, 32], BF16))
            raT = EC(nc.sbuf_tensor("raT", [32, 2 * 512], BF16))
            pYs = [PB0, PB1]
            pZ = PB1
            pH = PB2
            pA_ap = PB4[0:32, 0:512]
            pTc = PB3

            B_done = emit("v", lambda e: e.memset(stp[:, 254:255], 0.0),
                          x2_spill_all + hT_spill_all)
            wT_tok = [None] * NL
            qwt_rd = [B_done]
            sct_rd = [B_done]
            dqf_rd = B_done
            dqb_rd = B_done
            pTc_rd = B_done
            for l in range(NL):
                tev_last = None
                for ot in range(NJ):
                    sl = ot % 2
                    qa = qwt[:]
                    tlq = emit("q", lambda e, l=l, ot=ot, qa=qa: e.dma_start(
                        out=qa, in_=qw_h[l, ot * 128:(ot + 1) * 128, :]),
                        (qwt_rd[0],), dmach="qw")
                    sa = sct[:]
                    tls = emit("q", lambda e, l=l, ot=ot, sa=sa: e.dma_start(
                        out=sa, in_=sc_h[l, ot * 128:(ot + 1) * 128, :]),
                        (sct_rd[0],), dmach="sc2")
                    td1 = emit("v", lambda e, qa=qa: e.tensor_scalar(
                        out=dqf[:], in0=qa, scalar1=1.0 / 7.5, scalar2=-1.0,
                        op0=ALU.mult, op1=ALU.add), (tlq, dqf_rd))
                    qwt_rd[0] = td1
                    td2 = emit("v", lambda e, sa=sa: e.tensor_tensor(
                        out=dqb[:].rearrange("p (g j) -> p g j", j=16),
                        in0=dqf[:].rearrange("p (g j) -> p g j", j=16),
                        in1=sa.unsqueeze(2).broadcast_to([128, 64, 16]),
                        op=ALU.mult), (tls, td1, dqb_rd))
                    dqf_rd = td2
                    sct_rd[0] = td2
                    for j in range(NJ):
                        qsl = j % 4
                        ttr = emit("t", lambda e, j=j, qsl=qsl: e.transpose(
                            pTc[:, qsl * 128:(qsl + 1) * 128],
                            dqb[:, j * 128:(j + 1) * 128], ident[:]),
                            (td2, pTc_rd))
                        dst = wT[:, l * NJ * 1024 + j * 1024 + ot * 128:
                                 l * NJ * 1024 + j * 1024 + (ot + 1) * 128]
                        tev = emit("s", lambda e, dst=dst, qsl=qsl: e.copy(
                            out=dst, in_=pTc[:, qsl * 128:(qsl + 1) * 128]), (ttr,))
                        tev_last = tev
                        pTc_rd = tev
                    dqb_rd = tev_last
                wT_tok[l] = tev_last
                if l == 0:
                    snap51 = {e: len(ops[e]) for e in STREAMS}
            snap52 = {e: len(ops[e]) for e in STREAMS}
            laT_tok = [None] * NL
            la_rd = B_done
            for l in range(NL):
                tl = emit("q", lambda e, l=l: e.dma_start(
                    out=dqf[0:32, :], in_=la_h[l]), (la_rd, dqf_rd), dmach="lo")
                tc_ = emit("v", lambda e: e.tensor_copy(out=lab2[:], in_=dqf[0:32, :]),
                           (tl,))
                dqf_rd = tc_
                la_rd = tc_
                evs = []
                for j in range(NJ):
                    qsl = j % 4
                    ttr = emit("t", lambda e, j=j, qsl=qsl: e.transpose(
                        pTc[:, qsl * 32:(qsl + 1) * 32],
                        lab2[:, j * 128:(j + 1) * 128], ident[0:32, 0:32]),
                        (tc_,) + ((pTc_rd,) if qsl == 0 else ()))
                    if qsl == 3:
                        j0 = j - 3
                        dst = laT[:, l * 256 + j0 * 32: l * 256 + (j0 + 4) * 32]
                        tev = emit("s", lambda e, dst=dst: e.copy(
                            out=dst, in_=pTc[:, 0:128]), (ttr,))
                        evs.append(tev)
                        pTc_rd = tev
                la_rd = evs[-1]
                laT_tok[l] = evs[-1]
            snap53 = {e: len(ops[e]) for e in STREAMS}
            lbT_tok = [None] * NL
            lb_rd = B_done
            for l in range(NL):
                evs = []
                for ot in range(NJ):
                    tl = emit("q", lambda e, l=l, ot=ot: e.dma_start(
                        out=lb32[:], in_=lb_h[l, ot * 128:(ot + 1) * 128, :]),
                        (lb_rd,), dmach="lb")
                    tc_ = emit("v", lambda e: e.tensor_copy(out=lbb2[:], in_=lb32[:]),
                               (tl,))
                    qsl = ot % 4
                    ttr = emit("t", lambda e, qsl=qsl: e.transpose(
                        pTc[0:32, qsl * 128:(qsl + 1) * 128], lbb2[:], ident[:]),
                        (tc_,) + ((pTc_rd,) if qsl == 0 else ()))
                    lb_rd = ttr
                    if qsl == 3:
                        ot0 = ot - 3
                        dst = lbT[:, l * 1024 + ot0 * 128: l * 1024 + (ot0 + 4) * 128]
                        tev = emit("s", lambda e, dst=dst: e.copy(
                            out=dst, in_=pTc[0:32, 0:512]), (ttr,))
                        evs.append(tev)
                        pTc_rd = tev
                lbT_tok[l] = evs[-1]
            snap54 = {e: len(ops[e]) for e in STREAMS}
            bi_tok = [None] * NL
            br_rd = B_done
            for l in range(NL):
                tl = emit("q", lambda e, l=l: e.dma_start(
                    out=dqf[0:1, :], in_=bi_h[l]), (br_rd, dqf_rd), dmach="bi")
                tc_ = emit("v", lambda e, l=l: e.tensor_copy(
                    out=brow[0:1, l * 1024:(l + 1) * 1024], in_=dqf[0:1, :]), (tl,))
                br_rd = tc_
                dqf_rd = tc_
                bi_tok[l] = tc_

            snap50 = {e: len(ops[e]) for e in STREAMS}
            hTb_rd = B_done
            hTf_rd = B_done
            g_rd = B_done
            gT_rd = B_done
            z_rd = B_done
            raT_rd = [B_done, B_done]
            pY_rd = [B_done, B_done]
            pA_rd = B_done
            pZ_rd = [B_done, B_done]
            pH_rd = B_done
            gmb_rd = [B_done]
            btb_rd = [B_done]
            x2r_rd = [B_done]
            out_toks = []
            snap6l = []
            for c in range(NC5):
                t0 = c * 512
                hload = []
                for j in range(NJ):
                    dst = hTb[:, j * 512:(j + 1) * 512]
                    tl = emit("q", lambda e, j=j, t0=t0, dst=dst: e.dma_start(
                        out=dst, in_=hTs[j * 128:(j + 1) * 128, t0:t0 + 512]),
                        hT_spill_all + (hTb_rd if j == 0 else None,), dmach="hb")
                    hload.append(tl)
                twd = emit("s", lambda e: e.copy(out=hTf[:], in_=hTb[:]),
                           tuple(hload) + (hTf_rd,))
                hTb_rd = twd
                h_new_tok = twd
                hb_tok = tuple(hload)
                for blk in range(3):
                    l1, l2 = 2 * blk, 2 * blk + 1
                    tgm = emit("q", lambda e, blk=blk: e.dma_start(
                        out=gmb[:], in_=gm_h[blk].broadcast_to([128, 1024])),
                        (gmb_rd[0],), dmach="gb")
                    tbt = emit("q", lambda e, blk=blk: e.dma_start(
                        out=btb[:], in_=bt_h[blk].broadcast_to([128, 1024])),
                        (btb_rd[0],), dmach="gb")
                    tgm = tbt
                    tmm = None
                    for j in range(NJ):
                        tmm = emit("t", lambda e, j=j, l1=l1: e.matmul(
                            pA_ap, laT[:, l1 * 256 + j * 32: l1 * 256 + (j + 1) * 32],
                            hTb[:, j * 512:(j + 1) * 512],
                            start=(j == 0), stop=(j == NJ - 1)),
                            (laT_tok[l1], pA_rd) + hb_tok if j == 0 else ())
                    tra = emit("s", lambda e: e.copy(out=raT[:, 0:512], in_=pA_ap),
                               (tmm, raT_rd[0]))
                    pA_rd = tra
                    gelu_toks = []
                    for ssi in range(4):
                        pY = pYs[ssi % 2]
                        dep0 = (wT_tok[l1], tra, bi_tok[l1], t_ones1,
                                pY_rd[ssi % 2]) + hb_tok
                        if ssi % 2 == 1:
                            dep0 = dep0 + tuple(pZ_rd)
                        tmm = None
                        for oc in range(2):
                            for j in range(NJ):
                                tmm = emit("t", lambda e, j=j, oc=oc, ssi=ssi, l1=l1,
                                           pY=pY: e.matmul(
                                    pY[:, oc * 512:(oc + 1) * 512],
                                    hTb[:, j * 512 + ssi * 128: j * 512 + (ssi + 1) * 128],
                                    wT[:, l1 * NJ * 1024 + j * 1024 + oc * 512:
                                       l1 * NJ * 1024 + j * 1024 + (oc + 1) * 512],
                                    start=(j == 0), stop=False),
                                    dep0 if (oc == 0 and j == 0) else ())
                            tmm = emit("t", lambda e, oc=oc, ssi=ssi, l1=l1, pY=pY:
                                       e.matmul(
                                pY[:, oc * 512:(oc + 1) * 512],
                                raT[0:32, ssi * 128:(ssi + 1) * 128],
                                lbT[:, l1 * 1024 + oc * 512: l1 * 1024 + (oc + 1) * 512],
                                start=False, stop=False), ())
                            tmm = emit("t", lambda e, oc=oc, l1=l1, pY=pY: e.matmul(
                                pY[:, oc * 512:(oc + 1) * 512],
                                ones1[0:1, 0:128],
                                brow[0:1, l1 * 1024 + oc * 512: l1 * 1024 + (oc + 1) * 512],
                                start=False, stop=True), ())
                        m_ap, rs_ap, nm_ap, tm, trs_, tnm, reg, tsq = ln_stats(
                            pY[:], 1024, (tmm,))
                        ts1 = emit("v", lambda e, m_ap=m_ap, pY=pY: e.scalar_tensor_tensor(
                            out=zbuf[:], in0=pY[:], scalar=m_ap,
                            in1=gmb[:], op0=ALU.subtract, op1=ALU.mult),
                            (tmm, tgm, z_rd, tsq), strict=(tm,))
                        gmb_rd[0] = ts1
                        reg(ts1)
                        ts2 = emit("v", lambda e, rs_ap=rs_ap: e.scalar_tensor_tensor(
                            out=zbuf[:], in0=zbuf[:], scalar=rs_ap,
                            in1=btb[:], op0=ALU.mult, op1=ALU.add),
                            (tbt, ts1), strict=(trs_,))
                        btb_rd[0] = ts2
                        reg(ts2)
                        pY_rd[ssi % 2] = ts2
                        tg = emit("s", lambda e, ssi=ssi: e.activation(
                            out=g_sb[:, ssi * 1024:(ssi + 1) * 1024], in_=zbuf[:],
                            func=AF.Gelu), (ts2, g_rd if ssi == 0 else None))
                        z_rd = tg
                        gelu_toks.append(tg)
                    gt_evs = []
                    for j in range(NJ):
                        ttr = None
                        for ssi in range(4):
                            ttr = emit("t", lambda e, j=j, ssi=ssi: e.transpose(
                                pTc[:, ssi * 128:(ssi + 1) * 128],
                                g_sb[:, ssi * 1024 + j * 128: ssi * 1024 + (j + 1) * 128],
                                ident[:]),
                                (gelu_toks[ssi],) + ((pTc_rd, gT_rd) if (ssi == 0) else ()))
                        tev = emit("s", lambda e, j=j: e.copy(
                            out=gT[:, j * 512:(j + 1) * 512], in_=pTc[:, 0:512]), (ttr,))
                        pTc_rd = tev
                        gt_evs.append(tev)
                    g_rd = gt_evs[-1]
                    tmm = None
                    for j in range(NJ):
                        tmm = emit("t", lambda e, j=j, l2=l2: e.matmul(
                            pA_ap, laT[:, l2 * 256 + j * 32: l2 * 256 + (j + 1) * 32],
                            gT[:, j * 512:(j + 1) * 512],
                            start=(j == 0), stop=(j == NJ - 1)),
                            ((laT_tok[l2], pA_rd) + tuple(gt_evs)) if j == 0 else ())
                    tra2 = emit("s", lambda e: e.copy(out=raT[:, 512:1024], in_=pA_ap),
                                (tmm, raT_rd[1]))
                    pA_rd = tra2
                    hupd_toks = []
                    hb_new = []
                    for oo in range(NJ):
                        zsl = oo % 2
                        dep0 = (wT_tok[l2], tra2, bi_tok[l2], pZ_rd[zsl],
                                pY_rd[1]) + tuple(gt_evs)
                        tmm = None
                        for j in range(NJ):
                            tmm = emit("t", lambda e, j=j, oo=oo, l2=l2, zsl=zsl: e.matmul(
                                pZ[:, zsl * 512:(zsl + 1) * 512],
                                wT[:, l2 * NJ * 1024 + j * 1024 + oo * 128:
                                   l2 * NJ * 1024 + j * 1024 + (oo + 1) * 128],
                                gT[:, j * 512:(j + 1) * 512],
                                start=(j == 0), stop=False),
                                dep0 if j == 0 else ())
                        tmm = emit("t", lambda e, oo=oo, l2=l2, zsl=zsl: e.matmul(
                            pZ[:, zsl * 512:(zsl + 1) * 512],
                            lbT[:, l2 * 1024 + oo * 128: l2 * 1024 + (oo + 1) * 128],
                            raT[0:32, 512:1024],
                            start=False, stop=False), ())
                        tmm = emit("t", lambda e, oo=oo, l2=l2, zsl=zsl: e.matmul(
                            pZ[:, zsl * 512:(zsl + 1) * 512],
                            brow[0:1, l2 * 1024 + oo * 128: l2 * 1024 + (oo + 1) * 128],
                            ones1[0:1, 0:512],
                            start=False, stop=True), ())
                        thu = emit("v", lambda e, oo=oo, zsl=zsl: e.tensor_tensor(
                            out=hTf[:, oo * 512:(oo + 1) * 512],
                            in0=pZ[:, zsl * 512:(zsl + 1) * 512],
                            in1=hTf[:, oo * 512:(oo + 1) * 512], op=ALU.add),
                            (tmm, h_new_tok))
                        pZ_rd[zsl] = thu
                        hupd_toks.append(thu)
                        if blk < 2:
                            thb = emit("s", lambda e, oo=oo: e.copy(
                                out=hTb[:, oo * 512:(oo + 1) * 512],
                                in_=hTf[:, oo * 512:(oo + 1) * 512]), (thu, hTb_rd))
                            hb_new.append(thb)
                    raT_rd[0] = tmm
                    raT_rd[1] = tmm
                    h_new_tok = hupd_toks[-1]
                    if blk < 2:
                        hb_tok = tuple(hb_new)
                        hTb_rd = hb_new[-1]
                    else:
                        hTb_rd = h_new_tok
                    hTf_rd = h_new_tok
                for ssi in range(4):
                    osl = ssi % 2
                    row0 = t0 + ssi * 128
                    xa = x2res[:]
                    tlx = emit("q", lambda e, row0=row0, xa=xa: e.dma_start(
                        out=xa, in_=x2s[row0:row0 + 128, :]),
                        x2_spill_all + (x2r_rd[0],), dmach="x2r")
                    ttr = None
                    for j in range(NJ):
                        ttr = emit("t", lambda e, j=j, ssi=ssi: e.transpose(
                            pH[:, j * 128:(j + 1) * 128],
                            hTf[:, j * 512 + ssi * 128: j * 512 + (ssi + 1) * 128],
                            identf[:]),
                            tuple(hupd_toks) + ((pH_rd,) if j == 0 else ()))
                    m_ap, rs_ap, nm_ap, tm, trs_, tnm, reg, tsq = ln_stats(
                        pH[:], 1024, (ttr,))
                    tap = emit("s", lambda e, rs_ap=rs_ap, nm_ap=nm_ap: e.activation(
                        out=zbuf[:], in_=pH[:], func=AF.Identity,
                        scale=rs_ap, bias=nm_ap),
                        (tnm, ttr, z_rd), strict=(trs_,))
                    reg(tap)
                    pH_rd = tap
                    hTf_rd = tap
                    tadd = emit("v", lambda e, xa=xa: e.tensor_add(
                        out=xa, in0=zbuf[:], in1=xa), (tap, tlx))
                    z_rd = tadd
                    tout = emit("g", lambda e, row0=row0, xa=xa: e.dma_start(
                        out=out_h[row0:row0 + 128, :], in_=xa), (tadd,), dmach="go")
                    x2r_rd[0] = tout
                    out_toks.append(tout)
                snap6l.append({e: len(ops[e]) for e in STREAMS})
            waitop("g", (out_toks[-1],))

            if PH != 3:
                if PH >= 60:
                    snap = snap6l[PH - 60]
                elif PH >= 50:
                    snap = {50: snap50, 51: snap51, 52: snap52,
                            53: snap53, 54: snap54}[PH]
                elif PH >= 40:
                    snap = {40: snap40, 41: snap41, 42: snap42, 43: snap43,
                            44: snap44, 45: snap45, 46: snap46}[PH]
                    if PH == 40:
                        snap = dict(snap40)
                        snap["v"] += int(os.environ.get("KDV", "0"))
                        snap["s"] += int(os.environ.get("KDS", "0"))
                elif PH >= 30:
                    snap = {30: snap30, 31: snap31, 32: snap32, 33: snap33,
                            34: snap34, 35: snap35}[PH]
                elif PH >= 20:
                    snap = snap2l[PH - 20]
                else:
                    snap = {10: snap10, 11: snap11, 12: snap12, 13: snap13,
                            1: snap1, 2: snap2}[PH]
                for e in STREAMS:
                    del ops[e][snap[e]:]
                ngo = sum(1 for o in ops["g"] if o[2] == "go")
                ops["g"].append(((), (), "go", lambda e: e.dma_start(
                    out=out_h[0:128, 0:256], in_=stp[:]), 16))
                ops["g"].append(((("go", (ngo + 1) * 16),), (), None, None, 0))

            def run_stream(name, eng):
                hi = {}
                for ds, st, ch, fn, inc in ops[name]:
                    need = {}
                    for (pe, val) in ds + st:
                        if val > need.get(pe, 0):
                            need[pe] = val
                    for pe, val in need.items():
                        if hi.get(pe, 0) >= val:
                            continue
                        eng.wait_ge(sems[pe], val)
                        hi[pe] = val
                    if fn is not None:
                        fn(eng).then_inc(sems[ch], inc)

            with nc.Block() as blk:
                @blk.tensor
                def _(eng):
                    run_stream("t", eng)

                @blk.vector
                def _(eng):
                    run_stream("v", eng)

                @blk.scalar
                def _(eng):
                    run_stream("s", eng)

                @blk.gpsimd
                def _(eng):
                    run_stream("g", eng)

                @blk.sync
                def _(eng):
                    run_stream("q", eng)
    return nc


def _consts():
    ident = np.eye(128, dtype=np.float32)
    p = np.arange(128).reshape(128, 1)
    f = np.arange(384).reshape(1, 384)
    cmask = (f >= p + 128).astype(np.float32)
    return (ident.astype(ml_dtypes.bfloat16), ident,
            cmask.astype(ml_dtypes.bfloat16))


def kernel(L0_attn_x, L0_attn_Wq, L0_attn_Wk, L0_attn_Wv, L0_attn_Wo,
           L0_ffn_qweights, L0_ffn_scales, L0_ffn_biases, L0_ffn_lora_a,
           L0_ffn_lora_b, L0_ffn_ln_gamma, L0_ffn_ln_beta):
    B, S, Dm = L0_attn_x.shape
    assert (B, S, Dm) == (8, 4096, 1024)
    idb, idf, cm = _consts()
    nc = build(S)
    common = dict(
        wq=np.ascontiguousarray(L0_attn_Wq, np.float32),
        wk=np.ascontiguousarray(L0_attn_Wk, np.float32),
        wv=np.ascontiguousarray(L0_attn_Wv, np.float32),
        wo=np.ascontiguousarray(L0_attn_Wo, np.float32),
        qw=np.ascontiguousarray(L0_ffn_qweights, np.int32),
        sc=np.ascontiguousarray(L0_ffn_scales, np.float32),
        bi=np.ascontiguousarray(L0_ffn_biases, np.float32).reshape(6, 1, 1024),
        la=np.ascontiguousarray(L0_ffn_lora_a, np.float32),
        lb=np.ascontiguousarray(L0_ffn_lora_b, np.float32),
        gm=np.ascontiguousarray(L0_ffn_ln_gamma, np.float32).reshape(3, 1, 1024),
        bt=np.ascontiguousarray(L0_ffn_ln_beta, np.float32).reshape(3, 1, 1024),
        idb=idb, idf=idf, cm=cm,
    )
    in_maps = [dict(common, x=np.ascontiguousarray(L0_attn_x[b], np.float32))
               for b in range(8)]
    res = run_bass_kernel_spmd(nc, in_maps, core_ids=list(range(8)))
    return np.stack([np.asarray(r["out"], np.float32) for r in res.results], axis=0)


# version 08 (final)
# speedup vs baseline: 1.0223x; 1.0209x over previous
import os
import numpy as np
import ml_dtypes
import concourse.bass as bass
import concourse.mybir as mybir
from concourse.bass_utils import run_bass_kernel_spmd
from contextlib import ExitStack

F32 = mybir.dt.float32
BF16 = mybir.dt.bfloat16
I32 = mybir.dt.int32
AF = mybir.ActivationFunctionType
ALU = mybir.AluOpType
AX = mybir.AxisListType

D = 1024
NJ = 8
R = 32
NL = 6
EPS = 1e-5
ISQ2 = float(1.0 / np.sqrt(2.0))


def build(S):
    PH = int(os.environ.get("KPH", "3"))
    assert S % 512 == 0
    NT = S // 128
    NC2 = S // 256
    NC5 = S // 512
    nc = bass.Bass("TRN2", target_bir_lowering=False)

    x_h = nc.dram_tensor("x", [S, D], F32, kind="ExternalInput").ap()
    wq_h = nc.dram_tensor("wq", [D, D], F32, kind="ExternalInput").ap()
    wk_h = nc.dram_tensor("wk", [D, D], F32, kind="ExternalInput").ap()
    wv_h = nc.dram_tensor("wv", [D, D], F32, kind="ExternalInput").ap()
    wo_h = nc.dram_tensor("wo", [D, D], F32, kind="ExternalInput").ap()
    qw_h = nc.dram_tensor("qw", [NL, D, D], I32, kind="ExternalInput").ap()
    sc_h = nc.dram_tensor("sc", [NL, D, 64], F32, kind="ExternalInput").ap()
    bi_h = nc.dram_tensor("bi", [NL, 1, D], F32, kind="ExternalInput").ap()
    la_h = nc.dram_tensor("la", [NL, R, D], F32, kind="ExternalInput").ap()
    lb_h = nc.dram_tensor("lb", [NL, D, R], F32, kind="ExternalInput").ap()
    gm_h = nc.dram_tensor("gm", [3, 1, D], F32, kind="ExternalInput").ap()
    bt_h = nc.dram_tensor("bt", [3, 1, D], F32, kind="ExternalInput").ap()
    idb_h = nc.dram_tensor("idb", [128, 128], BF16, kind="ExternalInput").ap()
    idf_h = nc.dram_tensor("idf", [128, 128], F32, kind="ExternalInput").ap()
    cm_h = nc.dram_tensor("cm", [128, 384], BF16, kind="ExternalInput").ap()
    out_h = nc.dram_tensor("out", [S, D], F32, kind="ExternalOutput").ap()
    qTs = nc.dram_tensor("qTs", [D, S], BF16, kind="Internal").ap()
    vs_h = nc.dram_tensor("vs", [S, D], BF16, kind="Internal").ap()
    x2s = nc.dram_tensor("x2s", [S, D], F32, kind="Internal").ap()
    hTs = nc.dram_tensor("hTs", [D, S], BF16, kind="Internal").ap()

    STREAMS = ("t", "v", "s", "g", "q")
    DMACH = ("qc", "xt0", "xt1", "wf0", "wf1", "wo0", "wo1", "qb0", "qb1",
             "xr0", "xr1", "gsv0", "gsv1", "gsq0", "gsq1",
             "gv0", "gv1", "gv2", "gv3", "go",
             "sd0", "sd1", "hs0", "hs1", "qw", "sc2", "lo", "lb", "bi", "gb", "hb", "x2r")
    ops = {e: [] for e in STREAMS}
    cnt = {e: 0 for e in STREAMS + DMACH}

    def emit(stream, fn, deps=(), strict=(), dmach=None):
        ch = dmach or stream
        inc = 16 if dmach else 1
        cnt[ch] += 1
        tok = (ch, cnt[ch] * inc)
        ds = tuple(d for d in deps
                   if d is not None and (d[0] != ch or ch in ("t", "v", "s")))
        st = tuple(d for d in strict if d is not None)
        ops[stream].append((ds, st, ch, fn, inc))
        return tok

    def waitop(stream, deps):
        ops[stream].append((tuple(d for d in deps if d is not None), (), None, None, 0))

    NSLOT = 32
    slot_read_toks = [[] for _ in range(NSLOT)]
    slot_state = {"i": 0}

    with ExitStack() as OUTER:
        EO = OUTER.enter_context
        sems = {ch: EO(nc.semaphore("sem_" + ch)) for ch in cnt}
        ident = EO(nc.sbuf_tensor("ident", [128, 128], BF16))
        identf = EO(nc.sbuf_tensor("identf", [128, 128], F32))
        cmask = EO(nc.sbuf_tensor("cmask", [128, 384], BF16))
        stp = EO(nc.sbuf_tensor("stp", [128, NSLOT * 8], F32))
        sqd = EO(nc.sbuf_tensor("sqd", [128, 1024], BF16))
        ones1 = EO(nc.sbuf_tensor("ones1", [1, 512], BF16))
        onesc = EO(nc.sbuf_tensor("onesc", [128, 1], BF16))
        brow = EO(nc.sbuf_tensor("brow", [1, NL * 1024], BF16))
        laT = EO(nc.sbuf_tensor("laT", [128, NL * 256], BF16))
        lbT = EO(nc.sbuf_tensor("lbT", [32, NL * 1024], BF16))
        rsrow = EO(nc.sbuf_tensor("rsrow", [1, 256], F32))
        rs_col = EO(nc.sbuf_tensor("rs_col", [128, 2], F32))
        PB0 = EO(nc.psum_tensor("PB0", [128, 1024], F32))
        PB1 = EO(nc.psum_tensor("PB1", [128, 1024], F32))
        PB2 = EO(nc.psum_tensor("PB2", [128, 1024], F32))
        PB3 = EO(nc.psum_tensor("PB3", [128, 1024], BF16))
        PB4 = EO(nc.psum_tensor("PB4", [128, 512], F32))

        emit("q", lambda e: e.dma_start(out=ident[:], in_=idb_h), dmach="qc")
        emit("q", lambda e: e.dma_start(out=identf[:], in_=idf_h), dmach="qc")
        t_id = t_idf = t_cm = emit(
            "q", lambda e: e.dma_start(out=cmask[:], in_=cm_h), dmach="qc")
        t_ones1 = emit("v", lambda e: e.memset(ones1[:], 1.0))
        t_onesc = emit("v", lambda e: e.memset(onesc[:], 1.0))

        def ln_stats(src_ap, W, deps):
            k = slot_state["i"] % NSLOT
            slot_state["i"] += 1
            war = tuple(slot_read_toks[k])
            slot_read_toks[k] = []
            c0 = k * 8
            m_ap = stp[:, c0:c0 + 1]
            rs_ap = stp[:, c0 + 1:c0 + 2]
            nm_ap = stp[:, c0 + 2:c0 + 3]
            sq_ap = stp[:, c0 + 3:c0 + 4]
            m2_ap = stp[:, c0 + 4:c0 + 5]
            deps = tuple(deps)
            t1 = emit("v", lambda e: e.reduce_sum(out=m_ap, in_=src_ap, axis=AX.X),
                      deps + war)
            t2 = emit("s", lambda e: e.activation(out=sqd[:, 0:W], in_=src_ap,
                                                  func=AF.Square, accum_out=sq_ap),
                      deps + war + (t1,))
            t3 = emit("v", lambda e: e.tensor_scalar_mul(out=m_ap, in0=m_ap,
                                                         scalar1=1.0 / W),
                      strict=(t1,))
            t4 = emit("s", lambda e: e.square(out=m2_ap, in_=m_ap), (t3,), strict=(t2,))
            t5 = emit("v", lambda e: e.scalar_tensor_tensor(
                out=rs_ap, in0=sq_ap, scalar=1.0 / W, in1=m2_ap,
                op0=ALU.mult, op1=ALU.subtract), (t2, t4))
            t5b = emit("v", lambda e: e.tensor_scalar_add(out=rs_ap, in0=rs_ap,
                                                          scalar1=EPS),
                       strict=(t5,))
            t6 = emit("s", lambda e: e.activation(out=rs_ap, in_=rs_ap,
                                                  func=AF.Sqrt), (t5b,))
            t7 = emit("v", lambda e: e.reciprocal(out=rs_ap, in_=rs_ap), (t6,))
            t8 = emit("v", lambda e: e.scalar_tensor_tensor(
                out=nm_ap, in0=m_ap, scalar=-1.0, in1=rs_ap,
                op0=ALU.mult, op1=ALU.mult), (t3,), strict=(t7,))

            def reg(tok):
                slot_read_toks[k].append(tok)
            reg(t8)
            return m_ap, rs_ap, nm_ap, t3, t7, t8, reg, t2

        with ExitStack() as AB:
            EAB = AB.enter_context
            kT = EAB(nc.sbuf_tensor("kT", [128, NJ * S], BF16))
            kT_done = [None]

            with ExitStack() as A:
                EA = A.enter_context
                xnT = EA(nc.sbuf_tensor("xnT", [128, NJ * S], BF16))
                wf32 = EA(nc.sbuf_tensor("wf32", [128, 2 * 1024], F32))
                wbuf = EA(nc.sbuf_tensor("wbuf", [128, NJ * 1024], BF16))
                xtile = EA(nc.sbuf_tensor("xtile", [128, 2 * 1024], F32))
                xn_sb = EA(nc.sbuf_tensor("xn_sb", [128, 2 * 1024], BF16))
                vst = EA(nc.sbuf_tensor("vst", [128, 2 * 1024], BF16))
                qst = EA(nc.sbuf_tensor("qst", [128, 2 * 512], BF16))
                pP = PB0
                pTa = PB3

                xt_rd = [None, None]
                xn_rd = [None, None]
                pTa_rd = [None]
                xnT_tok = {}
                for tt in range(NT):
                    sl = tt % 2
                    xa = xtile[:, sl * 1024:(sl + 1) * 1024]
                    tld = emit("q", lambda e, tt=tt, xa=xa: e.dma_start(
                        out=xa, in_=x_h[tt * 128:(tt + 1) * 128, :]),
                        deps=(xt_rd[sl],), dmach="xt%d" % sl)
                    m_ap, rs_ap, nm_ap, tm, trs, tnm, reg, tsq = ln_stats(xa, 1024, (tld,))
                    na = xn_sb[:, sl * 1024:(sl + 1) * 1024]
                    tap = emit("s", lambda e, xa=xa, na=na, rs_ap=rs_ap, nm_ap=nm_ap:
                               e.activation(out=na, in_=xa, func=AF.Identity,
                                            scale=rs_ap, bias=nm_ap),
                               (tnm, xn_rd[sl]), strict=(trs,))
                    reg(tap)
                    xt_rd[sl] = tap
                    last_ev = None
                    for j in range(NJ):
                        psl = j % 2
                        pa = pTa[:, psl * 128:(psl + 1) * 128]
                        ttr = emit("t", lambda e, na=na, j=j, pa=pa: e.transpose(
                            pa, na[:, j * 128:(j + 1) * 128], ident[:]),
                            (tap, t_id, pTa_rd[0]))
                        dst = xnT[:, j * S + tt * 128: j * S + (tt + 1) * 128]
                        tev = emit("s", lambda e, pa=pa, dst=dst: e.copy(out=dst, in_=pa),
                                   (ttr,))
                        pTa_rd[0] = tev
                        xnT_tok[(j, tt)] = tev
                        last_ev = tev
                    xn_rd[sl] = last_ev
                    if tt == 0:
                        snap10 = {e: len(ops[e]) for e in STREAMS}
                xnT_all = last_ev

                def xnT_ready(tc_lo, tc_hi):
                    return xnT_tok[(NJ - 1, tc_hi - 1)]


                snap11 = {e: len(ops[e]) for e in STREAMS}
                wf_rd = [None, None]
                def load_wproj(w_hbm, deps):
                    toks = []
                    for j in range(NJ):
                        sl = j % 2
                        wa = wf32[:, sl * 1024:(sl + 1) * 1024]
                        tl = emit("q", lambda e, wa=wa, j=j, w_hbm=w_hbm: e.dma_start(
                            out=wa, in_=w_hbm[j * 128:(j + 1) * 128, :]),
                            deps=(wf_rd[sl],) + tuple(deps), dmach="wf%d" % sl)
                        tc = emit("v", lambda e, wa=wa, j=j: e.tensor_copy(
                            out=wbuf[:, j * 1024:(j + 1) * 1024], in_=wa), (tl,))
                        wf_rd[sl] = tc
                        toks.append(tc)
                    return toks

                wk_toks = load_wproj(wk_h, ())
                spill_fin = {}
                pP_rd = [None, None]
                kT_tok = {}
                for m in range(NJ):
                    for tc in range(S // 512):
                        psl = (m * (S // 512) + tc) % 2
                        pa = pP[:, psl * 512:(psl + 1) * 512]
                        dep0 = (xnT_tok[(NJ - 1, tc * 4 + 3)],
                                pP_rd[psl]) + tuple(wk_toks)
                        tmm = None
                        for j in range(NJ):
                            tmm = emit("t", lambda e, j=j, m=m, tc=tc, pa=pa: e.matmul(
                                pa,
                                wbuf[:, j * 1024 + m * 128: j * 1024 + (m + 1) * 128],
                                xnT[:, j * S + tc * 512: j * S + (tc + 1) * 512],
                                start=(j == 0), stop=(j == NJ - 1)),
                                dep0 if j == 0 else ())
                        dst = kT[:, m * S + tc * 512: m * S + (tc + 1) * 512]
                        tev = emit("s", lambda e, pa=pa, dst=dst: e.copy(out=dst, in_=pa),
                                   (tmm,))
                        pP_rd[psl] = tev
                        kT_tok[(m, tc)] = tev
                kT_done[0] = tev

                snap12 = {e: len(ops[e]) for e in STREAMS}
                wv_toks = load_wproj(wv_h, (kT_done[0],))
                vst_rd = [None, None]
                for tt in range(NT):
                    sl = tt % 2
                    for dv in range(2):
                        psl = dv
                        pa = pP[:, psl * 512:(psl + 1) * 512]
                        dep0 = (pP_rd[psl], vst_rd[sl]) + tuple(wv_toks)
                        tmm = None
                        for j in range(NJ):
                            tmm = emit("t", lambda e, j=j, tt=tt, dv=dv, pa=pa: e.matmul(
                                pa,
                                xnT[:, j * S + tt * 128: j * S + (tt + 1) * 128],
                                wbuf[:, j * 1024 + dv * 512: j * 1024 + (dv + 1) * 512],
                                start=(j == 0), stop=(j == NJ - 1)),
                                dep0 if j == 0 else ())
                        dst = vst[:, sl * 1024 + dv * 512: sl * 1024 + (dv + 1) * 512]
                        tev = emit("s", lambda e, pa=pa, dst=dst: e.copy(out=dst, in_=pa),
                                   (tmm,))
                        pP_rd[psl] = tev
                        if dv == 0:
                            tev0 = tev
                    tsp = emit("g", lambda e, tt=tt, sl=sl: e.dma_start(
                        out=vs_h[tt * 128:(tt + 1) * 128, :],
                        in_=vst[:, sl * 1024:(sl + 1) * 1024]), (tev0, tev),
                        dmach="gsv%d" % sl)
                    vst_rd[sl] = tsp
                    spill_fin[tsp[0]] = tsp
                v_done = tev

                snap13 = {e: len(ops[e]) for e in STREAMS}
                wq_toks = load_wproj(wq_h, (v_done,))
                qst_rd = [None, None]
                for m in range(NJ):
                    for tc in range(S // 512):
                        psl = (m * (S // 512) + tc) % 2
                        sl = (m * (S // 512) + tc) % 2
                        pa = pP[:, psl * 512:(psl + 1) * 512]
                        dep0 = (pP_rd[psl], qst_rd[sl]) + tuple(wq_toks)
                        tmm = None
                        for j in range(NJ):
                            tmm = emit("t", lambda e, j=j, m=m, tc=tc, pa=pa: e.matmul(
                                pa,
                                wbuf[:, j * 1024 + m * 128: j * 1024 + (m + 1) * 128],
                                xnT[:, j * S + tc * 512: j * S + (tc + 1) * 512],
                                start=(j == 0), stop=(j == NJ - 1)),
                                dep0 if j == 0 else ())
                        dst = qst[:, sl * 512:(sl + 1) * 512]
                        tev = emit("s", lambda e, pa=pa, dst=dst: e.copy(out=dst, in_=pa),
                                   (tmm,))
                        pP_rd[psl] = tev
                        tsp = emit("g", lambda e, m=m, tc=tc, dst=dst: e.dma_start(
                            out=qTs[m * 128:(m + 1) * 128, tc * 512:(tc + 1) * 512],
                            in_=dst), (tev,), dmach="gsq%d" % sl)
                        qst_rd[sl] = tsp
                        spill_fin[tsp[0]] = tsp
                spill_final = tuple(spill_fin.values())
            snap1 = {e: len(ops[e]) for e in STREAMS}
            with ExitStack() as B:
                EB = B.enter_context
                qbuf = EB(nc.sbuf_tensor("qbuf", [128, 2 * NJ * 256], BF16))
                vbuf = EB(nc.sbuf_tensor("vbuf", [128, 4 * 1024], BF16))
                expT = EB(nc.sbuf_tensor("expT", [128, 4 * 256], BF16))
                attn_sb = EB(nc.sbuf_tensor("attn_sb", [128, 2 * 1024], BF16))
                attnT = EB(nc.sbuf_tensor("attnT", [128, NJ * 256], BF16))
                wo32 = EB(nc.sbuf_tensor("wo32", [128, 2 * 1024], F32))
                wo_sb = EB(nc.sbuf_tensor("wo_sb", [128, NJ * 1024], BF16))
                x_res = EB(nc.sbuf_tensor("x_res", [128, 2 * 1024], F32))
                x2_sb = EB(nc.sbuf_tensor("x2_sb", [128, 2 * 1024], F32))
                x2n_sb = EB(nc.sbuf_tensor("x2n_sb", [128, 2 * 1024], BF16))
                xnT_st = EB(nc.sbuf_tensor("xnT_st", [128, 2 * NJ * 256], BF16))
                lnb = EB(nc.sbuf_tensor("lnb", [128, 1024], F32))
                pAV0 = PB0
                pAV1 = PB1
                pS = PB2
                pRS_ap = PB4[0:1, 256:512]
                pTb = PB3
                pTrs = PB4

                A_done = emit("v", lambda e: e.memset(stp[:, 255:256], 0.0),
                              spill_final + (kT_done[0],))

                wo_toks = []
                wo_rd2 = [A_done, A_done]
                for j in range(NJ):
                    sl = j % 2
                    wa = wo32[:, sl * 1024:(sl + 1) * 1024]
                    tl = emit("q", lambda e, wa=wa, j=j: e.dma_start(
                        out=wa, in_=wo_h[j * 128:(j + 1) * 128, :]),
                        deps=(wo_rd2[sl],), dmach="wo%d" % sl)
                    tc_ = emit("v", lambda e, wa=wa, j=j: e.tensor_copy(
                        out=wo_sb[:, j * 1024:(j + 1) * 1024], in_=wa), (tl,))
                    wo_rd2[sl] = tc_
                    wo_toks.append(tc_)

                pAVs = [pAV0, pAV1]
                qb_rd = [A_done, A_done]
                vb_rd = [A_done] * 4
                exp_rd = [A_done] * 4
                pS_rd = [A_done, A_done]
                pAV_rd = [A_done, A_done]
                pRS_rd = A_done
                rsrow_rd = A_done
                rscol_rd = A_done
                attn_rd = [A_done, A_done]
                attnT_rd = A_done
                xres_rd = [A_done, A_done]
                x2_rd = [A_done, A_done]
                x2sp_rd = [None, None]
                x2n_rd = [A_done, A_done]
                xst_rd = [A_done, A_done]
                lnb_rd = A_done
                vb_ctr = [0]
                snap2l = []
                x2_spill_toks = []
                hT_spill_toks = []

                for c in range(NC2):
                    s0 = c * 256
                    qsl = c % 2
                    q_toks = []
                    for m in range(NJ):
                        dst = qbuf[:, qsl * NJ * 256 + m * 256: qsl * NJ * 256 + (m + 1) * 256]
                        tq = emit("q", lambda e, m=m, s0=s0, dst=dst: e.dma_start(
                            out=dst, in_=qTs[m * 128:(m + 1) * 128, s0:s0 + 256]),
                            spill_final + (qb_rd[qsl],), dmach="qb%d" % qsl)
                        q_toks.append(tq)
                    n_tt = 2 * c + 2
                    t_score_last = None
                    t_rs_last = None
                    t_av_last = None
                    pend = None
                    for tt in range(n_tt + 1):
                        if tt < n_tt:
                            vsl = vb_ctr[0] % 4
                            vb_ctr[0] += 1
                            va = vbuf[:, vsl * 1024:(vsl + 1) * 1024]
                            tv = emit("g", lambda e, tt=tt, va=va: e.dma_start(
                                out=va, in_=vs_h[tt * 128:(tt + 1) * 128, :]),
                                spill_final + (vb_rd[vsl],), dmach="gv%d" % vsl)
                            psl = tt % 2
                            pa = pS[:, psl * 512: psl * 512 + 256]
                            dep0 = tuple(q_toks) + (pS_rd[psl], kT_done[0])
                            tmm = None
                            for m in range(NJ):
                                tmm = emit("t", lambda e, m=m, tt=tt, c=c, pa=pa,
                                           qsl=qsl: e.matmul(
                                    pa,
                                    kT[:, m * S + tt * 128: m * S + (tt + 1) * 128],
                                    qbuf[:, qsl * NJ * 256 + m * 256: qsl * NJ * 256 + (m + 1) * 256],
                                    start=(m == 0), stop=(m == NJ - 1)),
                                    dep0 if m == 0 else ())
                            t_score_last = tmm
                            esl = tt % 4
                            ea = expT[:, esl * 256:(esl + 1) * 256]
                            tex = emit("s", lambda e, pa=pa, ea=ea: e.activation(
                                out=ea, in_=pa, func=AF.Exp, scale=0.03125),
                                (tmm, exp_rd[esl]))
                            pS_rd[psl] = tex
                            tmask = None
                            if tt >= 2 * c:
                                kk = tt - 2 * c
                                cs = 128 - 128 * kk
                                tmask = emit("v", lambda e, ea=ea, cs=cs:
                                             e.tensor_tensor(
                                    out=ea, in0=ea, in1=cmask[:, cs:cs + 256],
                                    op=ALU.mult), (tex, t_cm))
                            cur = (tt, ea, va, tmask or tex, tv, vsl, esl)
                        else:
                            cur = None
                        if pend is not None:
                            ptt, ea, va, tconsume, tv, vsl, esl = pend
                            trs = emit("t", lambda e, ea=ea, ptt=ptt, n_tt=n_tt:
                                       e.matmul(
                                pRS_ap, onesc[:], ea,
                                start=(ptt == 0), stop=(ptt == n_tt - 1)),
                                (tconsume, t_onesc) + ((pRS_rd,) if ptt == 0 else ()))
                            t_rs_last = trs
                            tav = None
                            for ss in range(2):
                                for dv in range(2):
                                    dep = (tconsume, tv)
                                    if ptt == 0:
                                        dep = dep + (pAV_rd[ss],)
                                    tav = emit("t", lambda e, ea=ea, ss=ss, dv=dv,
                                               va=va, ptt=ptt, n_tt=n_tt: e.matmul(
                                        pAVs[ss][:, dv * 512:(dv + 1) * 512],
                                        ea[:, ss * 128:(ss + 1) * 128],
                                        va[:, dv * 512:(dv + 1) * 512],
                                        start=(ptt == 0), stop=(ptt == n_tt - 1)),
                                        dep)
                            t_av_last = tav
                            vb_rd[vsl] = tav
                            exp_rd[esl] = tav
                        pend = cur
                    qb_rd[qsl] = t_score_last
                    if c == 0:
                        snap30 = {e: len(ops[e]) for e in STREAMS}
                    trc = emit("v", lambda e: e.reciprocal(out=rsrow[:], in_=pRS_ap),
                               (t_rs_last, rsrow_rd))
                    ttr0 = emit("t", lambda e: e.transpose(
                        pTrs[:, 0:1], rsrow[0:1, 0:128], identf[0:1, 0:1]),
                        (trc, t_idf, rscol_rd))
                    ttr1 = emit("t", lambda e: e.transpose(
                        pTrs[:, 1:2], rsrow[0:1, 128:256], identf[0:1, 0:1]), (trc,))
                    tec = emit("v", lambda e: e.tensor_copy(out=rs_col[:],
                                                            in_=pTrs[:, 0:2]),
                               (ttr0, ttr1))
                    rsrow_rd = ttr1
                    pRS_rd = tec
                    if c == 0:
                        snap31 = {e: len(ops[e]) for e in STREAMS}
                    tproj = [None, None]
                    for ss in range(2):
                        aa = attn_sb[:, ss * 1024:(ss + 1) * 1024]
                        tn = emit("s", lambda e, ss=ss, aa=aa: e.activation(
                            out=aa, in_=pAVs[ss][:], func=AF.Copy,
                            scale=rs_col[:, ss:ss + 1]),
                            (t_av_last, attn_rd[ss]), strict=(tec,))
                        attn_rd[ss] = tn
                        pAV_rd[ss] = tn
                    rscol_rd = attn_rd[1]
                    if c == 0:
                        snap32 = {e: len(ops[e]) for e in STREAMS}
                    at_evs = []
                    for j in range(NJ):
                        tt0 = emit("t", lambda e, j=j: e.transpose(
                            pTb[:, 0:128],
                            attn_sb[:, 0 * 1024 + j * 128: 0 * 1024 + (j + 1) * 128],
                            ident[:]), (attn_rd[0], attnT_rd))
                        tt1 = emit("t", lambda e, j=j: e.transpose(
                            pTb[:, 128:256],
                            attn_sb[:, 1 * 1024 + j * 128: 1 * 1024 + (j + 1) * 128],
                            ident[:]), (attn_rd[1],))
                        tev = emit("v", lambda e, j=j: e.tensor_copy(
                            out=attnT[:, j * 256:(j + 1) * 256], in_=pTb[:, 0:256]),
                            (tt0, tt1))
                        at_evs.append(tev)
                        attnT_rd = tev
                    attn_rd[0] = at_evs[-1]
                    attn_rd[1] = at_evs[-1]
                    if c == 0:
                        snap33 = {e: len(ops[e]) for e in STREAMS}
                    for ss in range(2):
                        dep0 = tuple(at_evs) + tuple(wo_toks) + (pAV_rd[ss],)
                        tmm = None
                        for j in range(NJ):
                            for dc in range(2):
                                tmm = emit("t", lambda e, j=j, dc=dc, ss=ss: e.matmul(
                                    pAVs[ss][:, dc * 512:(dc + 1) * 512],
                                    attnT[:, j * 256 + ss * 128: j * 256 + (ss + 1) * 128],
                                    wo_sb[:, j * 1024 + dc * 512: j * 1024 + (dc + 1) * 512],
                                    start=(j == 0), stop=(j == NJ - 1)),
                                    dep0 if (j == 0 and dc == 0) else ())
                        tproj[ss] = tmm
                    if c == 0:
                        snap34 = {e: len(ops[e]) for e in STREAMS}
                    for ss in range(2):
                        row0 = s0 + ss * 128
                        xsl = ss
                        xa = x_res[:, xsl * 1024:(xsl + 1) * 1024]
                        tlx = emit("q", lambda e, row0=row0, xa=xa: e.dma_start(
                            out=xa, in_=x_h[row0:row0 + 128, :]),
                            (xres_rd[xsl],), dmach="xr%d" % xsl)
                        if c == 0 and ss == 0:
                            snap40 = {e: len(ops[e]) for e in STREAMS}
                        m_ap, rs_ap, nm_ap, tm, trs_, tnm, reg, tsq = ln_stats(
                            pAVs[ss][:], 1024, (tproj[ss],))
                        if c == 0 and ss == 0:
                            snap41 = {e: len(ops[e]) for e in STREAMS}
                        tap = emit("s", lambda e, ss=ss, rs_ap=rs_ap, nm_ap=nm_ap:
                                   e.activation(out=lnb[:], in_=pAVs[ss][:],
                                                func=AF.Identity, scale=rs_ap,
                                                bias=nm_ap),
                                   (tnm, lnb_rd), strict=(trs_,))
                        reg(tap)
                        pAV_rd[ss] = tap
                        if c == 0 and ss == 0:
                            snap42 = {e: len(ops[e]) for e in STREAMS}
                        x2a = x2_sb[:, ss * 1024:(ss + 1) * 1024]
                        tx2 = emit("v", lambda e, x2a=x2a, xa=xa: e.tensor_add(
                            out=x2a, in0=lnb[:], in1=xa),
                            (tap, tlx, x2_rd[ss], x2sp_rd[ss]))
                        lnb_rd = tx2
                        xres_rd[xsl] = tx2
                        if c == 0 and ss == 0:
                            snap43 = {e: len(ops[e]) for e in STREAMS}
                        tspx2 = emit("s", lambda e, row0=row0, x2a=x2a: e.dma_start(
                            out=x2s[row0:row0 + 128, :], in_=x2a), (tx2,),
                            dmach="sd%d" % ss)
                        x2_spill_toks.append(tspx2)
                        x2sp_rd[ss] = tspx2
                        if c == 0 and ss == 0:
                            snap44 = {e: len(ops[e]) for e in STREAMS}
                        m2_ap, rs2_ap, nm2_ap, tm2, trs2, tnm2, reg2, _ = ln_stats(
                            x2a, 1024, (tx2,))
                        if c == 0 and ss == 0:
                            snap45 = {e: len(ops[e]) for e in STREAMS}
                        xna = x2n_sb[:, ss * 1024:(ss + 1) * 1024]
                        txn = emit("v", lambda e, x2a=x2a, xna=xna, m2_ap=m2_ap,
                                   rs2_ap=rs2_ap: e.tensor_scalar(
                            out=xna, in0=x2a, scalar1=m2_ap, scalar2=rs2_ap,
                            op0=ALU.subtract, op1=ALU.mult),
                            (x2n_rd[ss],), strict=(tm2, trs2))
                        reg2(txn)
                        x2_rd[ss] = txn
                        if c == 0 and ss == 0:
                            snap46 = {e: len(ops[e]) for e in STREAMS}
                        x2n_rd[ss] = None
                    if c == 0:
                        snap35 = {e: len(ops[e]) for e in STREAMS}
                    stsl = c % 2
                    for j in range(NJ):
                        tt0 = emit("t", lambda e, j=j: e.transpose(
                            pTb[:, 0:128],
                            x2n_sb[:, 0 * 1024 + j * 128: 0 * 1024 + (j + 1) * 128],
                            ident[:]), (x2_rd[0], attnT_rd))
                        tt1 = emit("t", lambda e, j=j: e.transpose(
                            pTb[:, 128:256],
                            x2n_sb[:, 1 * 1024 + j * 128: 1 * 1024 + (j + 1) * 128],
                            ident[:]), (x2_rd[1],))
                        dst = xnT_st[:, stsl * NJ * 256 + j * 256: stsl * NJ * 256 + (j + 1) * 256]
                        tev = emit("v", lambda e, dst=dst: e.tensor_copy(
                            out=dst, in_=pTb[:, 0:256]),
                                   (tt0, tt1, xst_rd[stsl] if j == 0 else None))
                        attnT_rd = tev
                        tsp = emit("s", lambda e, j=j, s0=s0, dst=dst: e.dma_start(
                            out=hTs[j * 128:(j + 1) * 128, s0:s0 + 256], in_=dst),
                            (tev,), dmach="hs%d" % stsl)
                        hT_spill_toks.append(tsp)
                        if j == NJ - 1:
                            xst_rd[stsl] = tsp
                            x2n_rd[0] = tev
                            x2n_rd[1] = tev
                    snap2l.append({e: len(ops[e]) for e in STREAMS})
                snap2c = getattr(build, "_snap2c", None)
                x2_spill_all = tuple({t[0]: t for t in x2_spill_toks}.values())
                hT_spill_all = tuple({t[0]: t for t in hT_spill_toks}.values())
        snap2 = {e: len(ops[e]) for e in STREAMS}
        with ExitStack() as C:
            EC = C.enter_context
            wT = EC(nc.sbuf_tensor("wT", [128, NL * NJ * 1024], BF16))
            gmb = EC(nc.sbuf_tensor("gmb", [128, 1024], F32))
            btb = EC(nc.sbuf_tensor("btb", [128, 1024], F32))
            hTb = EC(nc.sbuf_tensor("hTb", [128, NJ * 512], BF16))
            hTf = EC(nc.sbuf_tensor("hTf", [128, NJ * 512], F32))
            g_sb = EC(nc.sbuf_tensor("g_sb", [128, 4 * 1024], BF16))
            gT = EC(nc.sbuf_tensor("gT", [128, NJ * 512], BF16))
            zbuf = EC(nc.sbuf_tensor("zbuf", [128, 1024], F32))
            x2res = EC(nc.sbuf_tensor("x2res", [128, 1024], F32))
            qwt = EC(nc.sbuf_tensor("qwt", [128, 1024], I32))
            dqf = EC(nc.sbuf_tensor("dqf", [128, 1024], F32))
            dqb = EC(nc.sbuf_tensor("dqb", [128, 1024], BF16))
            sct = EC(nc.sbuf_tensor("sct", [128, 64], F32))
            lab2 = EC(nc.sbuf_tensor("lab2", [32, 1024], BF16))
            lb32 = EC(nc.sbuf_tensor("lb32", [128, 32], F32))
            lbb2 = EC(nc.sbuf_tensor("lbb2", [128, 32], BF16))
            raT = EC(nc.sbuf_tensor("raT", [32, 2 * 512], BF16))
            pYs = [PB0, PB1]
            pZ = PB1
            pH = PB2
            pA_ap = PB4[0:32, 0:512]
            pTc = PB3

            B_done = emit("v", lambda e: e.memset(stp[:, 254:255], 0.0),
                          x2_spill_all + hT_spill_all)
            wT_tok = [None] * NL
            qwt_rd = [B_done]
            sct_rd = [B_done]
            dqf_rd = B_done
            dqb_rd = B_done
            pTc_rd = B_done
            for l in range(NL):
                tev_last = None
                for ot in range(NJ):
                    sl = ot % 2
                    qa = qwt[:]
                    tlq = emit("q", lambda e, l=l, ot=ot, qa=qa: e.dma_start(
                        out=qa, in_=qw_h[l, ot * 128:(ot + 1) * 128, :]),
                        (qwt_rd[0],), dmach="qw")
                    sa = sct[:]
                    tls = emit("q", lambda e, l=l, ot=ot, sa=sa: e.dma_start(
                        out=sa, in_=sc_h[l, ot * 128:(ot + 1) * 128, :]),
                        (sct_rd[0],), dmach="sc2")
                    td1 = emit("v", lambda e, qa=qa: e.tensor_scalar(
                        out=dqf[:], in0=qa, scalar1=1.0 / 7.5, scalar2=-1.0,
                        op0=ALU.mult, op1=ALU.add), (tlq, dqf_rd))
                    qwt_rd[0] = td1
                    td2 = emit("v", lambda e, sa=sa: e.tensor_tensor(
                        out=dqb[:].rearrange("p (g j) -> p g j", j=16),
                        in0=dqf[:].rearrange("p (g j) -> p g j", j=16),
                        in1=sa.unsqueeze(2).broadcast_to([128, 64, 16]),
                        op=ALU.mult), (tls, td1, dqb_rd))
                    dqf_rd = td2
                    sct_rd[0] = td2
                    for j in range(NJ):
                        qsl = j % 4
                        ttr = emit("t", lambda e, j=j, qsl=qsl: e.transpose(
                            pTc[:, qsl * 128:(qsl + 1) * 128],
                            dqb[:, j * 128:(j + 1) * 128], ident[:]),
                            (td2, pTc_rd))
                        dst = wT[:, l * NJ * 1024 + j * 1024 + ot * 128:
                                 l * NJ * 1024 + j * 1024 + (ot + 1) * 128]
                        tev = emit("s", lambda e, dst=dst, qsl=qsl: e.copy(
                            out=dst, in_=pTc[:, qsl * 128:(qsl + 1) * 128]), (ttr,))
                        tev_last = tev
                        pTc_rd = tev
                    dqb_rd = tev_last
                wT_tok[l] = tev_last
                if l == 0:
                    snap51 = {e: len(ops[e]) for e in STREAMS}
            snap52 = {e: len(ops[e]) for e in STREAMS}
            laT_tok = [None] * NL
            la_rd = B_done
            for l in range(NL):
                tl = emit("q", lambda e, l=l: e.dma_start(
                    out=dqf[0:32, :], in_=la_h[l]), (la_rd, dqf_rd), dmach="lo")
                tc_ = emit("v", lambda e: e.tensor_copy(out=lab2[:], in_=dqf[0:32, :]),
                           (tl,))
                dqf_rd = tc_
                la_rd = tc_
                evs = []
                for j in range(NJ):
                    qsl = j % 4
                    ttr = emit("t", lambda e, j=j, qsl=qsl: e.transpose(
                        pTc[:, qsl * 32:(qsl + 1) * 32],
                        lab2[:, j * 128:(j + 1) * 128], ident[0:32, 0:32]),
                        (tc_,) + ((pTc_rd,) if qsl == 0 else ()))
                    if qsl == 3:
                        j0 = j - 3
                        dst = laT[:, l * 256 + j0 * 32: l * 256 + (j0 + 4) * 32]
                        tev = emit("s", lambda e, dst=dst: e.copy(
                            out=dst, in_=pTc[:, 0:128]), (ttr,))
                        evs.append(tev)
                        pTc_rd = tev
                la_rd = evs[-1]
                laT_tok[l] = evs[-1]
            snap53 = {e: len(ops[e]) for e in STREAMS}
            lbT_tok = [None] * NL
            lb_rd = B_done
            for l in range(NL):
                evs = []
                for ot in range(NJ):
                    tl = emit("q", lambda e, l=l, ot=ot: e.dma_start(
                        out=lb32[:], in_=lb_h[l, ot * 128:(ot + 1) * 128, :]),
                        (lb_rd,), dmach="lb")
                    tc_ = emit("v", lambda e: e.tensor_copy(out=lbb2[:], in_=lb32[:]),
                               (tl,))
                    qsl = ot % 4
                    ttr = emit("t", lambda e, qsl=qsl: e.transpose(
                        pTc[0:32, qsl * 128:(qsl + 1) * 128], lbb2[:], ident[:]),
                        (tc_,) + ((pTc_rd,) if qsl == 0 else ()))
                    lb_rd = ttr
                    if qsl == 3:
                        ot0 = ot - 3
                        dst = lbT[:, l * 1024 + ot0 * 128: l * 1024 + (ot0 + 4) * 128]
                        tev = emit("s", lambda e, dst=dst: e.copy(
                            out=dst, in_=pTc[0:32, 0:512]), (ttr,))
                        evs.append(tev)
                        pTc_rd = tev
                lbT_tok[l] = evs[-1]
            snap54 = {e: len(ops[e]) for e in STREAMS}
            bi_tok = [None] * NL
            br_rd = B_done
            for l in range(NL):
                tl = emit("q", lambda e, l=l: e.dma_start(
                    out=dqf[0:1, :], in_=bi_h[l]), (br_rd, dqf_rd), dmach="bi")
                tc_ = emit("v", lambda e, l=l: e.tensor_copy(
                    out=brow[0:1, l * 1024:(l + 1) * 1024], in_=dqf[0:1, :]), (tl,))
                br_rd = tc_
                dqf_rd = tc_
                bi_tok[l] = tc_

            snap50 = {e: len(ops[e]) for e in STREAMS}
            hTb_rd = B_done
            hTf_rd = B_done
            g_rd = B_done
            gT_rd = B_done
            z_rd = B_done
            raT_rd = [B_done, B_done]
            pY_rd = [B_done, B_done]
            pA_rd = B_done
            pZ_rd = [B_done, B_done]
            pH_rd = B_done
            gmb_rd = [B_done]
            btb_rd = [B_done]
            x2r_rd = [B_done]
            out_toks = []
            snap6l = []
            for c in range(NC5):
                t0 = c * 512
                hload = []
                for j in range(NJ):
                    dst = hTb[:, j * 512:(j + 1) * 512]
                    tl = emit("q", lambda e, j=j, t0=t0, dst=dst: e.dma_start(
                        out=dst, in_=hTs[j * 128:(j + 1) * 128, t0:t0 + 512]),
                        hT_spill_all + (hTb_rd if j == 0 else None,), dmach="hb")
                    hload.append(tl)
                twd = emit("s", lambda e: e.copy(out=hTf[:], in_=hTb[:]),
                           tuple(hload) + (hTf_rd,))
                hTb_rd = twd
                h_new_tok = twd
                hb_tok = tuple(hload)
                for blk in range(3):
                    l1, l2 = 2 * blk, 2 * blk + 1
                    tgm = emit("q", lambda e, blk=blk: e.dma_start(
                        out=gmb[:], in_=gm_h[blk].broadcast_to([128, 1024])),
                        (gmb_rd[0],), dmach="gb")
                    tbt = emit("q", lambda e, blk=blk: e.dma_start(
                        out=btb[:], in_=bt_h[blk].broadcast_to([128, 1024])),
                        (btb_rd[0],), dmach="gb")
                    tgm = tbt
                    tmm = None
                    for j in range(NJ):
                        tmm = emit("t", lambda e, j=j, l1=l1: e.matmul(
                            pA_ap, laT[:, l1 * 256 + j * 32: l1 * 256 + (j + 1) * 32],
                            hTb[:, j * 512:(j + 1) * 512],
                            start=(j == 0), stop=(j == NJ - 1)),
                            (laT_tok[l1], pA_rd) + hb_tok if j == 0 else ())
                    tra = emit("s", lambda e: e.copy(out=raT[:, 0:512], in_=pA_ap),
                               (tmm, raT_rd[0]))
                    pA_rd = tra
                    gelu_toks = []
                    for ssi in range(4):
                        pY = pYs[ssi % 2]
                        dep0 = (wT_tok[l1], tra, bi_tok[l1], t_ones1,
                                pY_rd[ssi % 2]) + hb_tok
                        if ssi % 2 == 1:
                            dep0 = dep0 + tuple(pZ_rd)
                        tmm = None
                        for oc in range(2):
                            for j in range(NJ):
                                tmm = emit("t", lambda e, j=j, oc=oc, ssi=ssi, l1=l1,
                                           pY=pY: e.matmul(
                                    pY[:, oc * 512:(oc + 1) * 512],
                                    hTb[:, j * 512 + ssi * 128: j * 512 + (ssi + 1) * 128],
                                    wT[:, l1 * NJ * 1024 + j * 1024 + oc * 512:
                                       l1 * NJ * 1024 + j * 1024 + (oc + 1) * 512],
                                    start=(j == 0), stop=False),
                                    dep0 if (oc == 0 and j == 0) else ())
                            tmm = emit("t", lambda e, oc=oc, ssi=ssi, l1=l1, pY=pY:
                                       e.matmul(
                                pY[:, oc * 512:(oc + 1) * 512],
                                raT[0:32, ssi * 128:(ssi + 1) * 128],
                                lbT[:, l1 * 1024 + oc * 512: l1 * 1024 + (oc + 1) * 512],
                                start=False, stop=False), ())
                            tmm = emit("t", lambda e, oc=oc, l1=l1, pY=pY: e.matmul(
                                pY[:, oc * 512:(oc + 1) * 512],
                                ones1[0:1, 0:128],
                                brow[0:1, l1 * 1024 + oc * 512: l1 * 1024 + (oc + 1) * 512],
                                start=False, stop=True), ())
                        m_ap, rs_ap, nm_ap, tm, trs_, tnm, reg, tsq = ln_stats(
                            pY[:], 1024, (tmm,))
                        ts1 = emit("v", lambda e, m_ap=m_ap, pY=pY: e.scalar_tensor_tensor(
                            out=zbuf[:], in0=pY[:], scalar=m_ap,
                            in1=gmb[:], op0=ALU.subtract, op1=ALU.mult),
                            (tmm, tgm, z_rd, tsq), strict=(tm,))
                        gmb_rd[0] = ts1
                        reg(ts1)
                        ts2 = emit("v", lambda e, rs_ap=rs_ap: e.scalar_tensor_tensor(
                            out=zbuf[:], in0=zbuf[:], scalar=rs_ap,
                            in1=btb[:], op0=ALU.mult, op1=ALU.add),
                            (tbt, ts1), strict=(trs_,))
                        btb_rd[0] = ts2
                        reg(ts2)
                        pY_rd[ssi % 2] = ts2
                        tg = emit("s", lambda e, ssi=ssi: e.activation(
                            out=g_sb[:, ssi * 1024:(ssi + 1) * 1024], in_=zbuf[:],
                            func=AF.Gelu), (ts2, g_rd if ssi == 0 else None))
                        z_rd = tg
                        gelu_toks.append(tg)
                    gt_evs = []
                    for j in range(NJ):
                        ttr = None
                        for ssi in range(4):
                            ttr = emit("t", lambda e, j=j, ssi=ssi: e.transpose(
                                pTc[:, ssi * 128:(ssi + 1) * 128],
                                g_sb[:, ssi * 1024 + j * 128: ssi * 1024 + (j + 1) * 128],
                                ident[:]),
                                (gelu_toks[ssi],) + ((pTc_rd, gT_rd) if (ssi == 0) else ()))
                        tev = emit("s", lambda e, j=j: e.copy(
                            out=gT[:, j * 512:(j + 1) * 512], in_=pTc[:, 0:512]), (ttr,))
                        pTc_rd = tev
                        gt_evs.append(tev)
                    g_rd = gt_evs[-1]
                    tmm = None
                    for j in range(NJ):
                        tmm = emit("t", lambda e, j=j, l2=l2: e.matmul(
                            pA_ap, laT[:, l2 * 256 + j * 32: l2 * 256 + (j + 1) * 32],
                            gT[:, j * 512:(j + 1) * 512],
                            start=(j == 0), stop=(j == NJ - 1)),
                            ((laT_tok[l2], pA_rd) + tuple(gt_evs)) if j == 0 else ())
                    tra2 = emit("s", lambda e: e.copy(out=raT[:, 512:1024], in_=pA_ap),
                                (tmm, raT_rd[1]))
                    pA_rd = tra2
                    hupd_toks = []
                    hb_new = []
                    for oo in range(NJ):
                        zsl = oo % 2
                        dep0 = (wT_tok[l2], tra2, bi_tok[l2], pZ_rd[zsl],
                                pY_rd[1]) + tuple(gt_evs)
                        tmm = None
                        for j in range(NJ):
                            tmm = emit("t", lambda e, j=j, oo=oo, l2=l2, zsl=zsl: e.matmul(
                                pZ[:, zsl * 512:(zsl + 1) * 512],
                                wT[:, l2 * NJ * 1024 + j * 1024 + oo * 128:
                                   l2 * NJ * 1024 + j * 1024 + (oo + 1) * 128],
                                gT[:, j * 512:(j + 1) * 512],
                                start=(j == 0), stop=False),
                                dep0 if j == 0 else ())
                        tmm = emit("t", lambda e, oo=oo, l2=l2, zsl=zsl: e.matmul(
                            pZ[:, zsl * 512:(zsl + 1) * 512],
                            lbT[:, l2 * 1024 + oo * 128: l2 * 1024 + (oo + 1) * 128],
                            raT[0:32, 512:1024],
                            start=False, stop=False), ())
                        tmm = emit("t", lambda e, oo=oo, l2=l2, zsl=zsl: e.matmul(
                            pZ[:, zsl * 512:(zsl + 1) * 512],
                            brow[0:1, l2 * 1024 + oo * 128: l2 * 1024 + (oo + 1) * 128],
                            ones1[0:1, 0:512],
                            start=False, stop=True), ())
                        thu = emit("v", lambda e, oo=oo, zsl=zsl: e.tensor_tensor(
                            out=hTf[:, oo * 512:(oo + 1) * 512],
                            in0=pZ[:, zsl * 512:(zsl + 1) * 512],
                            in1=hTf[:, oo * 512:(oo + 1) * 512], op=ALU.add),
                            (tmm, h_new_tok))
                        pZ_rd[zsl] = thu
                        hupd_toks.append(thu)
                        if blk < 2:
                            thb = emit("s", lambda e, oo=oo: e.copy(
                                out=hTb[:, oo * 512:(oo + 1) * 512],
                                in_=hTf[:, oo * 512:(oo + 1) * 512]), (thu, hTb_rd))
                            hb_new.append(thb)
                    raT_rd[0] = tmm
                    raT_rd[1] = tmm
                    h_new_tok = hupd_toks[-1]
                    if blk < 2:
                        hb_tok = tuple(hb_new)
                        hTb_rd = hb_new[-1]
                    else:
                        hTb_rd = h_new_tok
                    hTf_rd = h_new_tok
                for ssi in range(4):
                    osl = ssi % 2
                    row0 = t0 + ssi * 128
                    xa = x2res[:]
                    tlx = emit("q", lambda e, row0=row0, xa=xa: e.dma_start(
                        out=xa, in_=x2s[row0:row0 + 128, :]),
                        x2_spill_all + (x2r_rd[0],), dmach="x2r")
                    ttr = None
                    for j in range(NJ):
                        ttr = emit("t", lambda e, j=j, ssi=ssi: e.transpose(
                            pH[:, j * 128:(j + 1) * 128],
                            hTf[:, j * 512 + ssi * 128: j * 512 + (ssi + 1) * 128],
                            identf[:]),
                            tuple(hupd_toks) + ((pH_rd,) if j == 0 else ()))
                    m_ap, rs_ap, nm_ap, tm, trs_, tnm, reg, tsq = ln_stats(
                        pH[:], 1024, (ttr,))
                    tap = emit("s", lambda e, rs_ap=rs_ap, nm_ap=nm_ap: e.activation(
                        out=zbuf[:], in_=pH[:], func=AF.Identity,
                        scale=rs_ap, bias=nm_ap),
                        (tnm, ttr, z_rd), strict=(trs_,))
                    reg(tap)
                    pH_rd = tap
                    hTf_rd = tap
                    tadd = emit("v", lambda e, xa=xa: e.tensor_add(
                        out=xa, in0=zbuf[:], in1=xa), (tap, tlx))
                    z_rd = tadd
                    tout = emit("g", lambda e, row0=row0, xa=xa: e.dma_start(
                        out=out_h[row0:row0 + 128, :], in_=xa), (tadd,), dmach="go")
                    x2r_rd[0] = tout
                    out_toks.append(tout)
                snap6l.append({e: len(ops[e]) for e in STREAMS})
            waitop("g", (out_toks[-1],))

            if PH != 3:
                if PH >= 60:
                    snap = snap6l[PH - 60]
                elif PH >= 50:
                    snap = {50: snap50, 51: snap51, 52: snap52,
                            53: snap53, 54: snap54}[PH]
                elif PH >= 40:
                    snap = {40: snap40, 41: snap41, 42: snap42, 43: snap43,
                            44: snap44, 45: snap45, 46: snap46}[PH]
                    if PH == 40:
                        snap = dict(snap40)
                        snap["v"] += int(os.environ.get("KDV", "0"))
                        snap["s"] += int(os.environ.get("KDS", "0"))
                elif PH >= 30:
                    snap = {30: snap30, 31: snap31, 32: snap32, 33: snap33,
                            34: snap34, 35: snap35}[PH]
                elif PH >= 20:
                    snap = snap2l[PH - 20]
                else:
                    snap = {10: snap10, 11: snap11, 12: snap12, 13: snap13,
                            1: snap1, 2: snap2}[PH]
                for e in STREAMS:
                    del ops[e][snap[e]:]
                ngo = sum(1 for o in ops["g"] if o[2] == "go")
                ops["g"].append(((), (), "go", lambda e: e.dma_start(
                    out=out_h[0:128, 0:256], in_=stp[:]), 16))
                ops["g"].append(((("go", (ngo + 1) * 16),), (), None, None, 0))

            def run_stream(name, eng):
                hi = {}
                for ds, st, ch, fn, inc in ops[name]:
                    need = {}
                    for (pe, val) in ds + st:
                        if val > need.get(pe, 0):
                            need[pe] = val
                    for pe, val in need.items():
                        if hi.get(pe, 0) >= val:
                            continue
                        eng.wait_ge(sems[pe], val)
                        hi[pe] = val
                    if fn is not None:
                        fn(eng).then_inc(sems[ch], inc)

            with nc.Block() as blk:
                @blk.tensor
                def _(eng):
                    run_stream("t", eng)

                @blk.vector
                def _(eng):
                    run_stream("v", eng)

                @blk.scalar
                def _(eng):
                    run_stream("s", eng)

                @blk.gpsimd
                def _(eng):
                    run_stream("g", eng)

                @blk.sync
                def _(eng):
                    run_stream("q", eng)
    return nc


def _consts():
    ident = np.eye(128, dtype=np.float32)
    p = np.arange(128).reshape(128, 1)
    f = np.arange(384).reshape(1, 384)
    cmask = (f >= p + 128).astype(np.float32)
    return (ident.astype(ml_dtypes.bfloat16), ident,
            cmask.astype(ml_dtypes.bfloat16))


def kernel(L0_attn_x, L0_attn_Wq, L0_attn_Wk, L0_attn_Wv, L0_attn_Wo,
           L0_ffn_qweights, L0_ffn_scales, L0_ffn_biases, L0_ffn_lora_a,
           L0_ffn_lora_b, L0_ffn_ln_gamma, L0_ffn_ln_beta):
    B, S, Dm = L0_attn_x.shape
    assert (B, S, Dm) == (8, 4096, 1024)
    idb, idf, cm = _consts()
    nc = build(S)
    common = dict(
        wq=np.ascontiguousarray(L0_attn_Wq, np.float32),
        wk=np.ascontiguousarray(L0_attn_Wk, np.float32),
        wv=np.ascontiguousarray(L0_attn_Wv, np.float32),
        wo=np.ascontiguousarray(L0_attn_Wo, np.float32),
        qw=np.ascontiguousarray(L0_ffn_qweights, np.int32),
        sc=np.ascontiguousarray(L0_ffn_scales, np.float32),
        bi=np.ascontiguousarray(L0_ffn_biases, np.float32).reshape(6, 1, 1024),
        la=np.ascontiguousarray(L0_ffn_lora_a, np.float32),
        lb=np.ascontiguousarray(L0_ffn_lora_b, np.float32),
        gm=np.ascontiguousarray(L0_ffn_ln_gamma, np.float32).reshape(3, 1, 1024),
        bt=np.ascontiguousarray(L0_ffn_ln_beta, np.float32).reshape(3, 1, 1024),
        idb=idb, idf=idf, cm=cm,
    )
    in_maps = [dict(common, x=np.ascontiguousarray(L0_attn_x[b], np.float32))
               for b in range(8)]
    res = run_bass_kernel_spmd(nc, in_maps, core_ids=list(range(8)))
    return np.stack([np.asarray(r["out"], np.float32) for r in res.results], axis=0)
